# version 14
# baseline (speedup 1.0000x reference)
import numpy as np
import concourse.bass as bass
import concourse.mybir as mybir
from concourse.bass_utils import run_bass_kernel_spmd
from concourse.alu_op_type import AluOpType as OP

F32 = mybir.dt.float32
AF = mybir.ActivationFunctionType
AX = mybir.AxisListType

B = 2
D = 1024
DH = 512
DIN = 1024
NH = 16
HDIM = 64
DST = 128
DCONV = 4
CONV_DIM = 1280
DIP = 2320
DFF = 2048
NEXP = 8
ETOT = 12
EPS = 1e-5
Q = 128
NCORES = 8
CAP = 1024


class Buf:
    __slots__ = ("writer", "readers", "name")

    def __init__(self, name=""):
        self.writer = None
        self.readers = set()
        self.name = name


class Sched:
    CKEYS = ("pe", "act", "dve", "gp")
    DKEYS = ("spd", "gpd")

    def __init__(self):
        self.ops = []
        self.fence_deps = []

    def op(self, key, fn, reads=(), writes=(), extra=()):
        dma = key in self.DKEYS
        psr = [b for b in reads if b is not None and b.name.startswith("ps")]
        if psr:
            reads = [b for b in reads if b is None or not b.name.startswith("ps")]
            writes = list(writes) + psr
        deps = set(self.fence_deps)
        for b in reads:
            if b is not None and b.writer is not None:
                deps.add(b.writer)
        for b in writes:
            if b is not None:
                deps.update(b.readers)
                if b.writer is not None:
                    deps.add(b.writer)
        for dd in extra:
            if dd is not None:
                deps.add(dd)
        i = len(self.ops)
        deps.discard(i)
        self.ops.append({"key": key, "fn": fn, "deps": sorted(deps), "dma": dma})
        for b in reads:
            if b is not None:
                b.readers.add(i)
        for b in writes:
            if b is not None:
                b.writer = i
                b.readers = set()
        return i

    def fence(self):
        last = {}
        for i, o in enumerate(self.ops):
            last[o["key"]] = i
        self.fence_deps = sorted(last.values())

    NDSEM = 12

    def emit(self, block, sems, dsems):
        import os as _os
        _mo = int(_os.environ.get("BASS_MAXOPS", "0"))
        ops = self.ops[:_mo] if _mo > 0 else self.ops
        needed = set()
        for o in ops:
            needed.update(o["deps"])
        incval = {}
        slot_of = {}
        ord_of = {}
        cnt = {k: 0 for k in self.CKEYS}
        dcnt = {k: 0 for k in self.DKEYS}
        for i, o in enumerate(ops):
            if o["dma"]:
                n = dcnt[o["key"]]
                dcnt[o["key"]] += 1
                slot_of[i] = n % self.NDSEM
                ord_of[i] = n // self.NDSEM + 1
            elif i in needed:
                cnt[o["key"]] += 1
                incval[i] = cnt[o["key"]]
        key2eng = {"pe": "tensor", "act": "scalar", "dve": "vector",
                   "gp": "gpsimd", "spd": "sync", "gpd": "gpsimd"}

        epos = {}
        ecnt = {}
        for i, o in enumerate(ops):
            if not o["dma"]:
                e = key2eng[o["key"]]
                epos[i] = ecnt.get(e, 0)
                ecnt[e] = epos[i] + 1

        def emit_for(eng_name, proxy):
            seen = {k: 0 for k in cnt}
            dseen = {(k, s): 0 for k in self.DKEYS for s in range(self.NDSEM)}
            for i, o in enumerate(ops):
                if key2eng[o["key"]] != eng_name:
                    continue
                if o["dma"]:
                    k, s, n = o["key"], slot_of[i], ord_of[i]
                    if n > 1 and dseen[(k, s)] < 16 * (n - 1):
                        proxy.wait_ge(dsems[k][s], 16 * (n - 1))
                        dseen[(k, s)] = 16 * (n - 1)
                for dd in o["deps"]:
                    do = ops[dd]
                    dk = do["key"]
                    if do["dma"]:
                        ks = (dk, slot_of[dd])
                        v = 16 * ord_of[dd]
                        if v > dseen[ks]:
                            proxy.wait_ge(dsems[dk][slot_of[dd]], v)
                            dseen[ks] = v
                    else:
                        if key2eng[dk] == eng_name:
                            if o["dma"] or eng_name == "gpsimd":
                                pass
                            elif eng_name == "tensor":
                                continue
                            elif epos[i] - epos[dd] > 1:
                                continue
                            v = incval[dd]
                            if v > seen[dk]:
                                proxy.wait_ge(sems[dk], v)
                                seen[dk] = v
                            continue
                        v = incval[dd]
                        if v > seen[dk]:
                            proxy.wait_ge(sems[dk], v)
                            seen[dk] = v
                h = o["fn"](proxy)
                if o["dma"]:
                    h.then_inc(dsems[o["key"]][slot_of[i]], 16)
                elif i in incval:
                    h.then_inc(sems[o["key"]], 1)

        @block.tensor
        def _(p):
            emit_for("tensor", p)

        @block.scalar
        def _(p):
            emit_for("scalar", p)

        @block.vector
        def _(p):
            emit_for("vector", p)

        @block.gpsimd
        def _(p):
            emit_for("gpsimd", p)

        @block.sync
        def _(p):
            emit_for("sync", p)


class Arena:
    def __init__(self, ap_full, ncols):
        self.t = ap_full
        self.ncols = ncols
        self.off = 0

    def take(self, cols):
        cols = int(cols)
        assert self.off + cols <= self.ncols, \
            f"arena overflow: {self.off}+{cols} > {self.ncols}"
        ap = self.t[:, self.off:self.off + cols]
        self.off += cols
        return ap

    def mark(self):
        return self.off

    def reset(self, mark):
        self.off = mark


def build_nc(tloc, ncores_run, phases="123"):
    TT = 256
    NTILE = tloc // TT
    NCH = tloc // Q
    CPT = TT // Q
    TT2 = 256
    NT2 = (B * tloc) // TT2

    nc = bass.Bass("TRN2", target_bir_lowering=False, detect_race_conditions=False)

    def din(name, shape):
        return nc.dram_tensor(name, list(shape), F32, kind="ExternalInput").ap()

    x_in = din("x", (B, tloc + 6, D))
    inw_in = {"f": din("inwT_f", (DH, DIP)), "b": din("inwT_b", (DH, DIP))}
    convw_in = {"f": din("convw_f", (CONV_DIM, DCONV)), "b": din("convw_b", (CONV_DIM, DCONV))}
    convb_in = {"f": din("convb_f", (128, 10)), "b": din("convb_b", (128, 10))}
    dtb_in = {"f": din("dtb_f", (NH, 1)), "b": din("dtb_b", (NH, 1))}
    alog_in = {"f": din("alog_f", (NH, 1)), "b": din("alog_b", (NH, 1))}
    dd_in = {"f": din("dd_f", (1, NH)), "b": din("dd_b", (1, NH))}
    gnw_in = {"f": din("gnw_f", (1, DIN)), "b": din("gnw_b", (1, DIN))}
    outpw_in = {"f": din("outpwT_f", (DIN, DH)), "b": din("outpwT_b", (DIN, DH))}
    normw_in = din("normw", (1, D))
    opw_in = din("opwT", (D, D))
    opb_in = din("opb", (128, 8))
    gatew_in = din("gatewT", (D, ETOT))
    ebias_in = din("ebias", (ETOT, 1))
    w1t_in = din("w1T", (NEXP, D, DFF))
    w2t_in = din("w2T", (NEXP, DFF, D))
    msnap_in = {"f": din("msnap_f", (128, 8)), "b": din("msnap_b", (128, 8))}

    out_ext = nc.dram_tensor("out", [B, tloc, D], F32, kind="ExternalOutput").ap()

    import os as _os
    _dbg = {"kind": "ExternalOutput"} if _os.environ.get("DEBUG_SC") else {}
    sc_xn = nc.dram_tensor("sc_xn", [8, 128, B * tloc], F32, **_dbg).ap()
    sc_z = nc.dram_tensor("sc_z", [4, 8, 128, tloc], F32, **_dbg).ap()
    sc_y = nc.dram_tensor("sc_y", [4, NCH, 128, DIN], F32, **_dbg).ap()
    sc_hl = nc.dram_tensor("sc_hl", [4, NCH, 128, DIN], F32, **_dbg).ap()
    sc_c = nc.dram_tensor("sc_c", [4, 128, tloc], F32, **_dbg).ap()
    sc_g = nc.dram_tensor("sc_g", [B * tloc, D], F32, **_dbg).ap()
    _dbgck = _os.environ.get("DEBUG_CHUNK")
    dbg_t = (nc.dram_tensor("dbg", [16, 128, 2048], F32, kind="ExternalOutput").ap()
             if _dbgck else None)
    eb_x = nc.dram_tensor("eb_x", [ETOT * CAP, D], F32).ap()
    eb_tw = nc.dram_tensor("eb_tw", [ETOT * CAP, 2], F32).ap()
    acc_h = nc.dram_tensor("acc_h", [B * tloc + ETOT * CAP, D], F32).ap()
    AGC = 4 * DIN + 64
    ag_in = nc.dram_tensor("ag_in", [128, AGC], F32).ap()
    ag_out = nc.dram_tensor("ag_out", [ncores_run * 128, AGC], F32).ap()

    rem = nc.sbuf_bytes_remaining // 4
    pers_cols = 7000 + 4 * NCH * NH
    main_cols = min(rem - pers_cols - 128, 47000)
    pers = Arena(nc.alloc_sbuf_tensor("pers", [128, pers_cols], F32).ap(), pers_cols)
    main = Arena(nc.alloc_sbuf_tensor("main", [128, main_cols], F32).ap(), main_cols)

    psum = [nc.alloc_psum_tensor(f"ps{i}", [128, 512], F32).ap() for i in range(8)]
    pb = [Buf(f"ps{i}") for i in range(8)]

    def pw(bank, col0, ncols):
        return [pb[bank]]

    S = Sched()
    bufs = {}
    bufs_zero_ops = []

    def B_(name):
        if name not in bufs:
            bufs[name] = Buf(name)
        return bufs[name]

    ones = pers.take(128)
    ident = pers.take(128)
    lt = pers.take(128)
    idxf = pers.take(128)
    idxp = pers.take(128)
    normw_bc = pers.take(D)
    gnw_bc = {"f": pers.take(DIN), "b": pers.take(DIN)}
    dcol = {"f": pers.take(NH), "b": pers.take(NH)}
    acol = {d: pers.take(1) for d in "fb"}
    dtbias = {d: pers.take(1) for d in "fb"}
    convw = {d: pers.take(4 * 10) for d in "fb"}
    convb = {d: pers.take(10) for d in "fb"}
    opb_c = pers.take(8)
    ebias_c = pers.take(1)
    ones16 = pers.take(NH)
    ltst = pers.take(128)
    iota12bc = pers.take(ETOT)
    zt = pers.take(D)
    tid_t = pers.take(192)
    epscol = pers.take(1)
    onecol = pers.take(1)
    gatew_s = pers.take(ETOT * 8)
    msnap_s = {d: pers.take(8) for d in "fb"}
    edel = {bd: [pers.take(NH) for _ in range(NCH)] for bd in range(4)}
    ebt = {bd: pers.take(NH) for bd in range(4)}
    bounds = {bd: pers.take(NCH) for bd in range(4)}
    gam_ag = pers.take(64)

    def setup_consts():
        S.op("dve", lambda v: v.memset(ones[:, :], 1.0), writes=[B_("ones")])
        S.op("dve", lambda v: v.memset(ones16[:, :], 1.0), writes=[B_("ones16")])
        S.op("dve", lambda v: v.memset(epscol[:, :], EPS), writes=[B_("epscol")])
        S.op("dve", lambda v: v.memset(onecol[:, :], 1.0), writes=[B_("onecol")])
        S.op("gp", lambda g: g.iota(idxf[:, :], [[1, 128]], base=0, channel_multiplier=0,
                                    allow_small_or_imprecise_dtypes=True),
             writes=[B_("idxf")])
        S.op("gp", lambda g: g.iota(idxp[:, :], [[0, 128]], base=0, channel_multiplier=1,
                                    allow_small_or_imprecise_dtypes=True),
             writes=[B_("idxp")])
        S.op("dve", lambda v: v.tensor_tensor(ident[:, :], idxf[:, :], idxp[:, :], OP.is_equal),
             reads=[B_("idxf"), B_("idxp")], writes=[B_("ident")])
        S.op("dve", lambda v: v.tensor_tensor(lt[:, :], idxf[:, :], idxp[:, :], OP.is_ge),
             reads=[B_("idxf"), B_("idxp")], writes=[B_("lt")])
        S.op("dve", lambda v: v.tensor_tensor(ltst[:, :], idxf[:, :], idxp[:, :], OP.is_gt),
             reads=[B_("idxf"), B_("idxp")], writes=[B_("ltst")])
        S.op("dve", lambda v: v.memset(zt[:, :], 0.0), writes=[B_("zt")])
        S.op("dve", lambda v: v.memset(tid_t[:, :], 0.0), writes=[B_("tid")])
        S.op("gp", lambda g: g.iota(tid_t[:, 0:192:2], [[128, 96]],
                                    base=B * tloc, channel_multiplier=1,
                                    allow_small_or_imprecise_dtypes=True),
             reads=[B_("tid")], writes=[B_("tid")])
        S.op("pe", lambda pe: pe.matmul(psum[7][:, 64:64 + ETOT], ones[0:1, :],
                                        idxf[0:1, 0:ETOT], start=True, stop=True),
             reads=[B_("idxf"), B_("ones")], writes=pw(7, 64, ETOT))
        S.op("dve", lambda v: v.tensor_copy(iota12bc[:, :], psum[7][:, 64:64 + ETOT]),
             reads=pw(7, 64, ETOT), writes=[B_("iota12")])
        zero_ops = []
        for blk in range(8 * CAP // 128):
            zero_ops.append(S.op("gpd", lambda p, blk=blk: p.dma_start(
                out=eb_x[128 * blk:128 * blk + 128, :], in_=zt[:, :]),
                reads=[B_("zt")], writes=[B_(f"ebz{blk}")]))
        zero_ops.append(S.op("gpd", lambda p: p.dma_start(
            out=eb_tw.rearrange("(c p) w -> p c w", p=128), in_=tid_t[:, :]),
            reads=[B_("tid")], writes=[B_("ebztw")]))
        bufs_zero_ops[:] = zero_ops
        for d in "fb":
            S.op("spd", lambda p, d=d: p.dma_start(out=acol[d][0:NH, :], in_=alog_in[d]),
                 writes=[B_("acol" + d)])
            S.op("spd", lambda p, d=d: p.dma_start(out=dtbias[d][0:NH, :], in_=dtb_in[d]),
                 writes=[B_("dtb" + d)])
            S.op("spd", lambda p, d=d: p.dma_start(out=msnap_s[d][:, :], in_=msnap_in[d]),
                 writes=[B_("msnap" + d)])
            for c in range(10):
                S.op("spd", lambda p, d=d, c=c: p.dma_start(
                    out=convw[d][:, 4 * c:4 * c + 4],
                    in_=convw_in[d][128 * c:128 * c + 128, :]),
                    writes=[B_("convw" + d)])
            S.op("spd", lambda p, d=d: p.dma_start(
                out=convb[d][:, :], in_=convb_in[d][:, :]),
                writes=[B_("convb" + d)])
            S.op("act", lambda a, d=d: a.activation(out=acol[d][0:NH, :],
                                                    in_=acol[d][0:NH, :], func=AF.Exp),
                 reads=[B_("acol" + d)], writes=[B_("acol" + d)])
            S.op("dve", lambda v, d=d: v.tensor_scalar(acol[d][0:NH, :], acol[d][0:NH, :],
                                                       -1.0, None, OP.mult),
                 reads=[B_("acol" + d)], writes=[B_("acol" + d)])
        S.op("spd", lambda p: p.dma_start(
            out=opb_c[:, :], in_=opb_in[:, :]),
            writes=[B_("opb")])
        for c in range(8):
            S.op("spd", lambda p, c=c: p.dma_start(
                out=gatew_s[:, ETOT * c:ETOT * c + ETOT],
                in_=gatew_in[128 * c:128 * c + 128, :]),
                writes=[B_("gatew")])
        S.op("spd", lambda p: p.dma_start(out=ebias_c[0:ETOT, :], in_=ebias_in),
             writes=[B_("ebias")])
        rows = [(normw_in, normw_bc, D, "normw")]
        for d in "fb":
            rows.append((gnw_in[d], gnw_bc[d], DIN, "gnw" + d))
            rows.append((dd_in[d], dcol[d], NH, "dcol" + d))
        for src, dst, n, nm in rows:
            rbuf = main.take(n)[0:1, :]
            S.op("spd", lambda p, src=src, rbuf=rbuf: p.dma_start(out=rbuf, in_=src),
                 writes=[B_("rowld_" + nm)])
            for i in range((n + 511) // 512):
                w = min(512, n - 512 * i)
                S.op("pe", lambda pe, rbuf=rbuf, i=i, w=w: pe.matmul(
                    psum[7][:, 0:w], ones[0:1, :], rbuf[:, 512 * i:512 * i + w],
                    start=True, stop=True),
                    reads=[B_("rowld_" + nm), B_("ones")], writes=pw(7, 0, w))
                S.op("dve", lambda v, dst=dst, i=i, w=w: v.tensor_copy(
                    dst[:, 512 * i:512 * i + w], psum[7][:, 0:w]),
                    reads=pw(7, 0, w), writes=[B_(nm)])
        main.reset(0)

    def phase1_pass(b, d):
        bd = 2 * b + (0 if d == "f" else 1)
        rev = (d == "b")
        hoff = 0 if d == "f" else DH
        m0 = main.mark()

        inw_s = [main.take(DIP) for _ in range(4)]
        x_tm = [[main.take(D) for _ in range(2)] for _ in range(2)]
        xa_fm = [[main.take(TT) for _ in range(4)] for _ in range(2)]
        xnf_b = [main.take(128) for _ in range(2)]
        ring = [[main.take(TT + 3) for _ in range(10)] for _ in range(2)]
        xbcs = [[main.take(TT) for _ in range(10)] for _ in range(2)]
        dtrow = [main.take(TT) for _ in range(2)]
        dtarow = [main.take(TT) for _ in range(2)]
        sp_t = [main.take(TT) for _ in range(2)]
        zbuf = [main.take(TT) for _ in range(2)]
        xh_tm = [main.take(DIN) for _ in range(2)]
        dtx_tm = [main.take(DIN) for _ in range(2)]
        dtxw = [main.take(DIN) for _ in range(2)]
        btm = [main.take(Q) for _ in range(2)]
        ebuf = [main.take(NH * Q) for _ in range(2)]
        gm = [main.take(Q) for _ in range(2)]
        _yb = main.take(DIN)
        ybuf = [_yb, _yb]
        dtm_c = [main.take(NH) for _ in range(2)]
        datm = [main.take(NH) for _ in range(2)]
        dtcol = [main.take(NH) for _ in range(2)]
        drows = [main.take(Q) for _ in range(2)]
        gcol = [main.take(NH) for _ in range(2)]
        grow = [main.take(NH) for _ in range(2)]
        s_run = main.take(DIN)
        trows = main.take(NH * Q)
        brun = main.take(1)
        ebx = main.take(NCH)
        nscr = main.take(D)
        ncols = [main.take(8) for _ in range(2)]
        hfm = main.take(16)

        bS = B_("srun")
        bBounds = B_(f"bounds{bd}")
        S.op("dve", lambda v: v.memset(s_run[:, :], 0.0), writes=[bS])

        for k in range(4):
            S.op("spd", lambda p, k=k: p.dma_start(
                out=inw_s[k][:, :], in_=inw_in[d][128 * k:128 * k + 128, :]),
                writes=[B_(f"inw{k}")])
        binw = [B_(f"inw{k}") for k in range(4)]

        def norms(xt, npart, NP, bX):
            xv = xt[0:npart, :]
            nn = ncols[NP][0:npart, :]
            sc = nscr[0:npart, :]
            bN = B_(f"ncols{NP}")
            bScr = B_("nscr")
            S.op("dve", lambda v: v.reduce_sum(nn[:, 0:1], xv, AX.X),
                 reads=[bX], writes=[bN])
            S.op("act", lambda a: a.activation(out=sc, in_=xv, func=AF.Square,
                                               accum_out=nn[:, 1:2]),
                 reads=[bX], writes=[bScr, bN])
            S.op("dve", lambda v: v.tensor_scalar(nn[:, 2:3], nn[:, 0:1],
                                                  1.0 / D, None, OP.mult),
                 reads=[bN], writes=[bN])
            S.op("dve", lambda v: v.tensor_tensor(nn[:, 4:5], nn[:, 2:3], nn[:, 2:3],
                                                  OP.mult),
                 reads=[bN], writes=[bN])
            S.op("dve", lambda v: v.tensor_scalar(nn[:, 3:4], nn[:, 1:2],
                                                  1.0 / D, nn[:, 4:5],
                                                  OP.mult, OP.subtract),
                 reads=[bN], writes=[bN])
            S.op("act", lambda a: a.activation(out=nn[:, 5:6], in_=nn[:, 3:4],
                                               func=AF.Sqrt, bias=epscol[0:npart, :]),
                 reads=[bN, B_("epscol")], writes=[bN])
            S.op("dve", lambda v: v.reciprocal(nn[:, 5:6], nn[:, 5:6]),
                 reads=[bN], writes=[bN])
            S.op("dve", lambda v: v.tensor_scalar(xv, xv, nn[:, 2:3], nn[:, 5:6],
                                                  OP.subtract, OP.mult),
                 reads=[bN], writes=[bX])
            S.op("act", lambda a: a.activation(out=sc, in_=xv, func=AF.Square,
                                               accum_out=nn[:, 6:7]),
                 reads=[bX], writes=[bScr, bN])
            S.op("act", lambda a: a.activation(out=nn[:, 7:8], in_=nn[:, 6:7],
                                               func=AF.Sqrt, scale=1.0 / D,
                                               bias=epscol[0:npart, :]),
                 reads=[bN, B_("epscol")], writes=[bN])
            S.op("dve", lambda v: v.reciprocal(nn[:, 7:8], nn[:, 7:8]),
                 reads=[bN], writes=[bN])

        hrow0 = 0 if not rev else tloc + 3
        bH = B_("xtm0_0")
        S.op("spd", lambda p: p.dma_start(out=x_tm[0][0][0:3, :],
                                          in_=x_in[b, hrow0:hrow0 + 3, :]),
             writes=[bH])
        norms(x_tm[0][0], 3, 0, bH)
        S.op("dve", lambda v: v.scalar_tensor_tensor(
            x_tm[0][0][0:3, hoff:hoff + DH], x_tm[0][0][0:3, hoff:hoff + DH],
            ncols[0][0:3, 7:8], normw_bc[0:3, hoff:hoff + DH], OP.mult, OP.mult),
            reads=[B_("ncols0"), B_("normw")], writes=[bH])
        for k in range(4):
            S.op("pe", lambda pe, k=k: pe.transpose(
                psum[4][:, 32 * k:32 * k + 3],
                x_tm[0][0][0:3, hoff + 128 * k:hoff + 128 * k + 128], ident[0:3, 0:3]),
                reads=[bH, B_("ident")], writes=pw(4, 32 * k, 3))
        for k in range(4):
            S.op("dve", lambda v, k=k: v.tensor_copy(hfm[:, 4 * k:4 * k + 3],
                                                     psum[4][:, 32 * k:32 * k + 3]),
                 reads=pw(4, 32 * k, 3), writes=[B_("hfm")])
        for c in range(10):
            bank = c % 2
            srcAP = hfm if not rev else None
            for k in range(4):
                S.op("pe", lambda pe, c=c, k=k, bank=bank: pe.matmul(
                    psum[bank][:, 0:3],
                    inw_s[k][:, 1024 + 128 * c:1024 + 128 * c + 128],
                    hfm[:, 4 * k:4 * k + 3], start=(k == 0), stop=(k == 3)),
                    reads=[B_("hfm"), binw[k]], writes=pw(bank, 0, 3))
            dst = ring[0][c][:, 0:3]
            if rev:
                dst = dst[:, ::-1]
            S.op("dve", lambda v, c=c, dst=dst, bank=bank: v.tensor_copy(
                dst, psum[bank][:, 0:3]),
                reads=pw(bank, 0, 3), writes=[B_(f"ring0_{c}")])

        if _dbgck and _dbgck.endswith(",h") and int(_dbgck.split(",")[0]) == bd:
            for di, (bn, ap) in enumerate([("hfm", hfm[:, :]), ("xtm0", x_tm[0][0][0:3, :]),
                                           ("ring8", ring[8][:, 0:8]), ("ring0", ring[0][:, 0:8]),
                                           ("ncols", ncols[0:3, :])]):
                S.op("gpd", lambda p, di=di, ap=ap: p.dma_start(
                    out=dbg_t[di, 0:ap.shape[0], 0:ap.shape[-1]], in_=ap),
                    reads=[B_(bn)], writes=[B_(f"dbgh{di}")])

        def chunk(ti, cc):
            ch = CPT * ti + cc
            t0 = Q * cc
            P = ti % 2
            cp = cc % 2
            bXB = [B_(f"xbcs{P}_{c}") for c in range(10)]
            S.op("pe", lambda pe: pe.transpose(psum[4][:, 0:128],
                                               xbcs[P][8][:, t0:t0 + Q], ident[:, :]),
                 reads=[bXB[8], B_("ident")], writes=pw(4, 0, 128))
            S.op("dve", lambda v: v.tensor_copy(btm[cp][:, :], psum[4][:, 0:128]),
                 reads=pw(4, 0, 128), writes=[B_(f"btm{cp}")])
            S.op("pe", lambda pe: pe.transpose(psum[4][:, 128:128 + NH],
                                               dtrow[P][0:NH, t0:t0 + Q], ident[0:NH, 0:NH]),
                 reads=[B_(f"dtrow{P}"), B_("ident")], writes=pw(4, 128, NH))
            S.op("dve", lambda v: v.tensor_copy(dtcol[cp][:, :], psum[4][:, 128:128 + NH]),
                 reads=pw(4, 128, NH), writes=[B_(f"dtcol{cp}")])
            S.op("pe", lambda pe: pe.transpose(psum[4][:, 256:256 + NH],
                                               dtarow[P][0:NH, t0:t0 + Q], ident[0:NH, 0:NH]),
                 reads=[B_(f"dtarow{P}"), B_("ident")], writes=pw(4, 256, NH))
            S.op("dve", lambda v: v.tensor_copy(datm[cp][:, :], psum[4][:, 256:256 + NH]),
                 reads=pw(4, 256, NH), writes=[B_(f"datm{cp}")])
            S.op("pe", lambda pe: pe.matmul(psum[6][:, 0:NH], lt[:, :], datm[cp][:, :],
                                            start=True, stop=True),
                 reads=[B_(f"datm{cp}"), B_("lt")], writes=pw(6, 0, NH))
            S.op("dve", lambda v: v.tensor_copy(dtm_c[cp][:, :], psum[6][:, 0:NH]),
                 reads=pw(6, 0, NH), writes=[B_(f"dtm{cp}")])
            S.op("act", lambda a: a.activation(out=edel[bd][ch][:, :], in_=psum[6][:, 0:NH],
                                               func=AF.Exp),
                 reads=pw(6, 0, NH), writes=[B_(f"edel{bd}_{ch}")])
            S.op("pe", lambda pe: pe.transpose(psum[6][0:NH, 128:256], dtm_c[cp][:, :],
                                               ident[:, :]),
                 reads=[B_(f"dtm{cp}"), B_("ident")], writes=pw(6, 128, 128))
            S.op("dve", lambda v: v.tensor_copy(drows[cp][0:NH, :], psum[6][0:NH, 128:256]),
                 reads=pw(6, 128, 128), writes=[B_(f"drows{cp}")])
            if ch == 0:
                S.op("dve", lambda v: v.tensor_copy(brun[0:NH, :], drows[cp][0:NH, 127:128]),
                     reads=[B_(f"drows{cp}")], writes=[B_("brun")])
            else:
                S.op("dve", lambda v: v.tensor_tensor(brun[0:NH, :], brun[0:NH, :],
                                                      drows[cp][0:NH, 127:128], OP.add),
                     reads=[B_(f"drows{cp}"), B_("brun")], writes=[B_("brun")])
            S.op("dve", lambda v, ch=ch: v.tensor_copy(bounds[bd][0:NH, ch:ch + 1],
                                                       brun[0:NH, :]),
                 reads=[B_("brun")], writes=[bBounds])
            S.op("pe", lambda pe: pe.transpose(psum[7][0:1, 32:32 + NH],
                                               drows[cp][0:NH, 127:128],
                                               ident[0:NH, 0:NH]),
                 reads=[B_(f"drows{cp}"), B_("ident")], writes=pw(7, 32, NH))
            S.op("act", lambda a: a.activation(out=grow[cp][0:1, 0:NH],
                                               in_=psum[7][0:1, 32:32 + NH],
                                               func=AF.Exp),
                 reads=pw(7, 32, NH), writes=[B_(f"grow{cp}")])
            S.op("pe", lambda pe: pe.matmul(psum[7][:, 0:NH], ones[0:1, :],
                                            grow[cp][0:1, 0:NH], start=True, stop=True),
                 reads=[B_(f"grow{cp}"), B_("ones")], writes=pw(7, 0, NH))
            S.op("dve", lambda v: v.tensor_copy(gcol[cp][:, :], psum[7][:, 0:NH]),
                 reads=pw(7, 0, NH), writes=[B_(f"gcol{cp}")])
            for fc in range(8):
                bank = 4 + (fc % 2)
                slot = 256
                S.op("pe", lambda pe, fc=fc, bank=bank, slot=slot: pe.transpose(
                    psum[bank][:, slot:slot + 128], xbcs[P][fc][:, t0:t0 + Q], ident[:, :]),
                    reads=[bXB[fc], B_("ident")], writes=pw(bank, slot, 128))
                S.op("dve", lambda v, fc=fc, bank=bank, slot=slot: v.tensor_copy(
                    xh_tm[cp][:, 128 * fc:128 * fc + 128], psum[bank][:, slot:slot + 128]),
                    reads=pw(bank, slot, 128), writes=[B_(f"xh{cp}")])
            S.op("dve", lambda v: v.tensor_tensor(
                dtx_tm[cp].rearrange("p (h q) -> p h q", h=NH),
                xh_tm[cp].rearrange("p (h q) -> p h q", h=NH),
                dtcol[cp].unsqueeze(2).broadcast_to((128, NH, HDIM)), OP.mult),
                reads=[B_(f"xh{cp}"), B_(f"dtcol{cp}")], writes=[B_(f"dtx{cp}")])
            S.op("spd", lambda p: p.dma_start(
                out=trows[0:1, 0:NH * Q], in_=drows[cp][0:NH, 0:Q]),
                reads=[B_(f"drows{cp}")], writes=[B_("trows")])
            for half in range(2):
                for j2 in range(2):
                    bank = 2 + j2
                    S.op("pe", lambda pe, half=half, j2=j2, bank=bank: pe.matmul(
                        psum[bank][:, 0:512], ones[0:1, :],
                        trows[0:1, 1024 * half + 512 * j2:1024 * half + 512 * j2 + 512],
                        start=True, stop=True),
                        reads=[B_("trows"), B_("ones")], writes=pw(bank, 0, 512))
                for j2 in range(2):
                    bank = 2 + j2
                    h0 = 8 * half + 4 * j2
                    S.op("dve", lambda v, half=half, j2=j2, bank=bank, h0=h0: v.tensor_tensor(
                        ebuf[cp][:, 512 * (2 * half + j2):512 * (2 * half + j2) + 512]
                            .rearrange("p (h t) -> p h t", h=4),
                        psum[bank][:, 0:512].rearrange("p (h t) -> p h t", h=4),
                        dtm_c[cp][:, h0:h0 + 4].unsqueeze(2).broadcast_to((128, 4, 128)),
                        OP.subtract),
                        reads=pw(bank, 0, 512) + [B_(f"dtm{cp}")], writes=[B_(f"ebuf{cp}")])
            S.op("dve", lambda v: v.tensor_scalar(
                ebuf[cp][:, :], ebuf[cp][:, :], 0.0, None, OP.min),
                reads=[B_(f"ebuf{cp}")], writes=[B_(f"ebuf{cp}")])
            S.op("act", lambda a: a.activation(out=ebuf[cp][:, :], in_=ebuf[cp][:, :], func=AF.Exp),
                 reads=[B_(f"ebuf{cp}")], writes=[B_(f"ebuf{cp}")])
            S.op("dve", lambda v: v.tensor_tensor(
                dtxw[cp].rearrange("p (h q) -> p h q", h=NH),
                dtx_tm[cp].rearrange("p (h q) -> p h q", h=NH),
                ebuf[cp][:, 127::128].unsqueeze(2).broadcast_to((128, NH, HDIM)),
                OP.mult),
                reads=[B_(f"dtx{cp}"), B_(f"ebuf{cp}")], writes=[B_(f"dtxw{cp}")])
            S.op("pe", lambda pe: pe.matmul(psum[6][:, 256:384], xbcs[P][8][:, t0:t0 + Q],
                                            xbcs[P][9][:, t0:t0 + Q], start=True, stop=True),
                 reads=[bXB[8], bXB[9]], writes=pw(6, 256, 128))
            S.op("dve", lambda v: v.tensor_tensor(gm[cp][:, :], psum[6][:, 256:384],
                                                  lt[:, :], OP.mult),
                 reads=pw(6, 256, 128) + [B_("lt")], writes=[B_(f"gm{cp}")])
            S.op("dve", lambda v: v.tensor_tensor(
                ebuf[cp].rearrange("p (h t) -> p h t", h=NH),
                ebuf[cp].rearrange("p (h t) -> p h t", h=NH),
                gm[cp].unsqueeze(1).broadcast_to((128, NH, Q)), OP.mult),
                reads=[B_(f"ebuf{cp}"), B_(f"gm{cp}")], writes=[B_(f"ebuf{cp}")])
            S.op("gpd", lambda p, ch=ch: p.dma_start(out=sc_hl[bd, ch, :, :],
                                                     in_=s_run[:, :]),
                 reads=[bS], writes=[B_(f"schl{bd}_{ch}")])
            for half in range(2):
                S.op("pe", lambda pe, half=half: pe.matmul(
                    psum[half][:, 0:512], btm[cp][:, :], dtxw[cp][:, 512 * half:512 * half + 512],
                    start=True, stop=True),
                    reads=[B_(f"btm{cp}"), B_(f"dtxw{cp}")], writes=pw(half, 0, 512))
            S.op("dve", lambda v: v.tensor_tensor(
                s_run.rearrange("p (h q) -> p h q", h=NH),
                s_run.rearrange("p (h q) -> p h q", h=NH),
                gcol[cp].unsqueeze(2).broadcast_to((128, NH, HDIM)), OP.mult),
                reads=[B_(f"gcol{cp}")], writes=[bS])
            for half in range(2):
                S.op("dve", lambda v, half=half: v.tensor_tensor(
                    s_run[:, 512 * half:512 * half + 512],
                    s_run[:, 512 * half:512 * half + 512],
                    psum[half][:, 0:512], OP.add),
                    reads=pw(half, 0, 512), writes=[bS])
            for h in range(NH):
                bank = 2 + h // 8
                off = 64 * (h % 8)
                S.op("pe", lambda pe, h=h, bank=bank, off=off: pe.matmul(
                    psum[bank][:, off:off + 64], ebuf[cp][:, 128 * h:128 * h + 128],
                    dtx_tm[cp][:, 64 * h:64 * h + 64],
                    start=(h % 8 == 0), stop=(h % 8 == 7)),
                    reads=[B_(f"ebuf{cp}"), B_(f"dtx{cp}")],
                    writes=pw(bank, 0, 512) if h % 8 == 0 else pw(bank, off, 64))
            S.op("dve", lambda v: v.tensor_tensor(
                ybuf[cp].rearrange("p (h q) -> p h q", h=NH),
                xh_tm[cp].rearrange("p (h q) -> p h q", h=NH),
                dcol[d].unsqueeze(2).broadcast_to((128, NH, HDIM)), OP.mult),
                reads=[B_(f"xh{cp}"), B_("dcol" + d)], writes=[B_("ybuf")])
            for half in range(2):
                bank = 2 + half
                S.op("dve", lambda v, half=half, bank=bank: v.tensor_tensor(
                    ybuf[cp][:, 512 * half:512 * half + 512],
                    ybuf[cp][:, 512 * half:512 * half + 512],
                    psum[bank][:, 0:512], OP.add),
                    reads=pw(bank, 0, 512), writes=[B_("ybuf")])
            S.op("gpd", lambda p, ch=ch: p.dma_start(out=sc_y[bd, ch, :, :],
                                                     in_=ybuf[cp][:, :]),
                 reads=[B_("ybuf")], writes=[B_(f"scy{bd}_{ch}")])
        def tile_body(ti):
            P = ti % 2
            for k in range(CPT):
                sblk = CPT * ti + k
                oblk = sblk if not rev else (tloc // 128 - 1 - sblk)
                r0 = 3 + 128 * oblk
                bX = B_(f"xtm{P}_{k}")
                S.op("spd", lambda p, k=k, r0=r0: p.dma_start(
                    out=x_tm[P][k][:, :], in_=x_in[b, r0:r0 + 128, :]), writes=[bX])
                norms(x_tm[P][k], 128, P, bX)
                if d == "f":
                    gtok = b * tloc + 128 * oblk
                    for fc in range(8):
                        bank = 4 + (fc % 2)
                        S.op("pe", lambda pe, fc=fc, k=k, bank=bank: pe.transpose(
                            psum[bank][:, 0:128],
                            x_tm[P][k][:, 128 * fc:128 * fc + 128], ident[:, :]),
                            reads=[bX, B_("ident")], writes=pw(bank, 0, 128))
                        S.op("dve", lambda v, fc=fc, bank=bank: v.tensor_copy(
                            xnf_b[P][:, :], psum[bank][:, 0:128]),
                            reads=pw(bank, 0, 128), writes=[B_(f"xnf{P}")])
                        S.op("gpd", lambda p, fc=fc, gtok=gtok: p.dma_start(
                            out=sc_xn[fc, :, gtok:gtok + 128], in_=xnf_b[P][:, :]),
                            reads=[B_(f"xnf{P}")], writes=[B_(f"scxn{fc}")])
                S.op("dve", lambda v, k=k: v.scalar_tensor_tensor(
                    x_tm[P][k][:, hoff:hoff + DH], x_tm[P][k][:, hoff:hoff + DH],
                    ncols[P][:, 7:8], normw_bc[:, hoff:hoff + DH], OP.mult, OP.mult),
                    reads=[B_(f"ncols{P}"), B_("normw")], writes=[bX])
                for fc in range(4):
                    bank = 4 + (fc % 2)
                    S.op("pe", lambda pe, fc=fc, k=k, bank=bank: pe.transpose(
                        psum[bank][:, 128:256],
                        x_tm[P][k][:, hoff + 128 * fc:hoff + 128 * fc + 128], ident[:, :]),
                        reads=[bX, B_("ident")], writes=pw(bank, 128, 128))
                    dst = xa_fm[P][fc][:, 128 * k:128 * k + 128]
                    if rev:
                        dst = dst[:, ::-1]
                    S.op("dve", lambda v, dst=dst, bank=bank: v.tensor_copy(
                        dst, psum[bank][:, 128:256]),
                        reads=pw(bank, 128, 128), writes=[B_(f"xafm{P}_{fc}")])

            rb = [B_(f"xafm{P}_{fc}") for fc in range(4)] + binw
            for oc in range(19):
                pn = oc % 4
                nr = 128 if oc < 18 else NH
                for kk in range(4):
                    S.op("pe", lambda pe, oc=oc, kk=kk, pn=pn, nr=nr: pe.matmul(
                        psum[pn][0:nr, 0:TT],
                        inw_s[kk][:, 128 * oc:128 * oc + nr],
                        xa_fm[P][kk][:, :], start=(kk == 0), stop=(kk == 3)),
                        reads=rb, writes=pw(pn, 0, TT))
                if oc < 8:
                    S.op("act", lambda a, pn=pn: a.activation(
                        out=zbuf[P][:, :], in_=psum[pn][:, 0:TT], func=AF.Silu),
                        reads=pw(pn, 0, TT), writes=[B_(f"zbuf{P}")])
                    S.op("gpd", lambda p, oc=oc, ti=ti: p.dma_start(
                        out=sc_z[bd, oc, :, TT * ti:TT * ti + TT], in_=zbuf[P][:, :]),
                        reads=[B_(f"zbuf{P}")], writes=[B_(f"scz{bd}_{oc}")])
                elif oc < 18:
                    c = oc - 8
                    S.op("dve", lambda v, c=c, pn=pn: v.tensor_copy(
                        ring[P][c][:, 3:3 + TT], psum[pn][:, 0:TT]),
                        reads=pw(pn, 0, TT), writes=[B_(f"ring{c}")])
                else:
                    pbr = pw(pn, 0, TT)
                    S.op("act", lambda a, pn=pn: a.activation(
                        out=sp_t[0][0:NH, :], in_=psum[pn][0:NH, 0:TT],
                        func=AF.Abs, bias=dtbias[d][0:NH, :]),
                        reads=pbr + [B_("dtb" + d)], writes=[B_("sp0")])
                    S.op("act", lambda a: a.activation(
                        out=sp_t[0][0:NH, :], in_=sp_t[0][0:NH, :], func=AF.Exp,
                        scale=-1.0),
                        reads=[B_("sp0")], writes=[B_("sp0")])
                    S.op("act", lambda a: a.activation(
                        out=sp_t[0][0:NH, :], in_=sp_t[0][0:NH, :], func=AF.Ln,
                        bias=onecol[0:NH, :]),
                        reads=[B_("sp0"), B_("onecol")], writes=[B_("sp0")])
                    S.op("act", lambda a, pn=pn: a.activation(
                        out=sp_t[1][0:NH, :], in_=psum[pn][0:NH, 0:TT],
                        func=AF.Relu, bias=dtbias[d][0:NH, :]),
                        reads=pbr + [B_("dtb" + d)], writes=[B_("sp1")])
                    S.op("dve", lambda v: v.tensor_tensor(
                        dtrow[P][0:NH, :], sp_t[0][0:NH, :], sp_t[1][0:NH, :], OP.add),
                        reads=[B_("sp0"), B_("sp1")], writes=[B_(f"dtrow{P}")])
            for c in range(10):
                rbc = B_(f"ring{P}_{c}")
                xb = B_(f"xbcs{P}_{c}")
                S.op("dve", lambda v, c=c: v.tensor_scalar(
                    xbcs[P][c][:, :], ring[P][c][:, 0:TT], convw[d][:, 4 * c:4 * c + 1],
                    convb[d][:, c:c + 1], OP.mult, OP.add),
                    reads=[rbc, B_("convw" + d), B_("convb" + d)], writes=[xb])
                for kk in range(1, 4):
                    S.op("dve", lambda v, c=c, kk=kk: v.scalar_tensor_tensor(
                        xbcs[P][c][:, :], ring[P][c][:, kk:kk + TT],
                        convw[d][:, 4 * c + kk:4 * c + kk + 1], xbcs[P][c][:, :],
                        OP.mult, OP.add),
                        reads=[rbc, B_("convw" + d), xb], writes=[xb])
                S.op("act", lambda a, c=c: a.activation(
                    out=xbcs[P][c][:, :], in_=xbcs[P][c][:, :], func=AF.Silu),
                    reads=[xb], writes=[xb])
                S.op("dve", lambda v, c=c, P=P: v.tensor_copy(ring[1 - P][c][:, 0:3],
                                                         ring[P][c][:, TT:TT + 3]),
                     reads=[rbc], writes=[B_(f"ring{1 - P}_{c}")])
            S.op("gpd", lambda p, ti=ti: p.dma_start(
                out=sc_c[bd, :, TT * ti:TT * ti + TT], in_=xbcs[P][9][:, :]),
                reads=[B_(f"xbcs{P}_9")], writes=[B_(f"scc{bd}")])
            S.op("dve", lambda v: v.tensor_scalar(
                dtarow[P][0:NH, :], dtrow[P][0:NH, :], acol[d][0:NH, :], None, OP.mult),
                reads=[B_(f"dtrow{P}"), B_("acol" + d)], writes=[B_(f"dtarow{P}")])
            for cc in range(CPT):
                chunk(ti, cc)

        for ti in range(NTILE):
            tile_body(ti)

        S.op("act", lambda a: a.activation(out=ebt[bd][0:NH, 0:NCH],
                                           in_=bounds[bd][0:NH, 0:NCH], func=AF.Exp),
             reads=[bBounds], writes=[B_(f"ebt{bd}")])
        S.op("pe", lambda pe: pe.transpose(psum[6][0:1, 64:64 + NH],
                                           ebt[bd][0:NH, NCH - 1:NCH],
                                           ident[0:NH, 0:NH]),
             reads=[B_(f"ebt{bd}"), B_("ident")], writes=pw(6, 64, NH))
        S.op("dve", lambda v: v.tensor_copy(grow[0][0:1, 0:NH], psum[6][0:1, 64:64 + NH]),
             reads=pw(6, 64, NH), writes=[B_("grow0")])
        S.op("pe", lambda pe: pe.matmul(psum[6][:, 64:64 + NH], ones[0:1, :],
                                        grow[0][0:1, 0:NH], start=True, stop=True),
             reads=[B_("grow0"), B_("ones")], writes=pw(6, 64, NH))
        S.op("dve", lambda v: v.tensor_copy(gam_ag[:, 16 * bd:16 * bd + 16],
                                            psum[6][:, 64:64 + NH]),
             reads=pw(6, 64, NH), writes=[B_("gamag")])
        S.op("gpd", lambda p: p.dma_start(out=ag_in[:, DIN * bd:DIN * bd + DIN],
                                          in_=s_run[:, :]),
             reads=[bS], writes=[B_("agin")])
        main.reset(m0)

    def phase2():
        NB = TT2 // 128
        m0 = main.mark()
        xn_s = [main.take(TT2) for _ in range(8)]
        xn_tm = [main.take(D) for _ in range(NB)]
        lgf = main.take(TT2)
        ltm = main.take(ETOT)
        tcol = main.take(8)
        wvt = main.take(ETOT)
        eqt = [main.take(ETOT) for _ in range(2)]
        maskp = main.take(ETOT)
        posg = main.take(ETOT)
        runbc = main.take(ETOT)
        roff = main.take(4)
        rrow = main.take(ETOT)
        destf = main.take(8)
        dscr = main.take(ETOT)
        idxu = main.take(2)
        tokw = [main.take(2) for _ in range(2)]
        acct = main.take(D)

        S.op("dve", lambda v: v.memset(roff[0:ETOT, 0:1], 0.0), writes=[B_("roff")])
        scat_ops = []
        init_ops = []
        for t2 in range(NT2):
            g0 = TT2 * t2
            bXNs = [B_(f"p2xn{fc}") for fc in range(8)]
            for fc in range(8):
                S.op("spd", lambda p, fc=fc, g0=g0: p.dma_start(
                    out=xn_s[fc][:, :], in_=sc_xn[fc, :, g0:g0 + TT2]),
                    reads=[B_(f"scxn{fc}")], writes=[bXNs[fc]])
            for tc in range(NB):
                for fc in range(8):
                    bank = 4 + fc % 2
                    S.op("pe", lambda pe, tc=tc, fc=fc, bank=bank: pe.transpose(
                        psum[bank][:, 0:128], xn_s[fc][:, 128 * tc:128 * tc + 128],
                        ident[:, :]),
                        reads=[bXNs[fc], B_("ident")], writes=pw(bank, 0, 128))
                    S.op("dve", lambda v, tc=tc, fc=fc, bank=bank: v.tensor_copy(
                        xn_tm[tc][:, 128 * fc:128 * fc + 128], psum[bank][:, 0:128]),
                        reads=pw(bank, 0, 128), writes=[B_(f"xntm{tc}")])
            for kc in range(8):
                S.op("pe", lambda pe, kc=kc: pe.matmul(
                    psum[6][0:ETOT, 0:TT2], gatew_s[:, ETOT * kc:ETOT * kc + ETOT],
                    xn_s[kc][:, :], start=(kc == 0), stop=(kc == 7)),
                    reads=[bXNs[kc], B_("gatew")], writes=pw(6, 0, TT2))
            S.op("dve", lambda v: v.tensor_scalar(
                lgf[0:ETOT, :], psum[6][0:ETOT, 0:TT2], ebias_c[0:ETOT, :], None, OP.add),
                reads=pw(6, 0, TT2) + [B_("ebias")], writes=[B_("lgf")])
            for kb in range(NB):
                gblk = NB * t2 + kb
                S.op("pe", lambda pe, kb=kb: pe.transpose(
                    psum[7][:, 0:ETOT], lgf[0:ETOT, 128 * kb:128 * kb + 128],
                    ident[0:ETOT, 0:ETOT]),
                    reads=[B_("lgf"), B_("ident")], writes=pw(7, 0, ETOT))
                S.op("dve", lambda v: v.tensor_copy(ltm[:, 0:ETOT], psum[7][:, 0:ETOT]),
                     reads=pw(7, 0, ETOT), writes=[B_("ltm")])
                lv = ltm[:, 0:ETOT]
                bL, bT, bW = B_("ltm"), B_("tcol"), B_("wvt")
                S.op("dve", lambda v, lv=lv: v.reduce_max(tcol[:, 0:1], lv, AX.X),
                     reads=[bL], writes=[bT])
                S.op("dve", lambda v, lv=lv: v.tensor_scalar(
                    eqt[0][:, :], lv, tcol[:, 0:1], None, OP.is_equal),
                    reads=[bL, bT], writes=[B_("eq0")])
                S.op("dve", lambda v, lv=lv: v.scalar_tensor_tensor(
                    wvt[:, :], eqt[0][:, :], -1e30, lv, OP.mult, OP.add),
                    reads=[bL, B_("eq0")], writes=[bW])
                S.op("dve", lambda v: v.reduce_max(tcol[:, 1:2], wvt[:, :], AX.X),
                     reads=[bW], writes=[bT])
                S.op("dve", lambda v, lv=lv: v.tensor_scalar(
                    eqt[1][:, :], lv, tcol[:, 1:2], None, OP.is_equal),
                    reads=[bL, bT], writes=[B_("eq1")])
                S.op("dve", lambda v: v.tensor_scalar(
                    tcol[:, 2:3], tcol[:, 0:1], tcol[:, 1:2], None, OP.subtract),
                    reads=[bT], writes=[bT])
                S.op("act", lambda a: a.activation(out=tcol[:, 3:4], in_=tcol[:, 2:3],
                                                   func=AF.Sigmoid),
                     reads=[bT], writes=[bT])
                S.op("dve", lambda v: v.tensor_scalar(
                    tcol[:, 4:5], tcol[:, 3:4], -1.0, 1.0, OP.mult, OP.add),
                    reads=[bT], writes=[bT])
                S.op("dve", lambda v: v.tensor_tensor(maskp[:, :], eqt[0][:, :],
                                                      eqt[1][:, :], OP.add),
                     reads=[B_("eq0"), B_("eq1")], writes=[B_("maskp")])
                S.op("pe", lambda pe: pe.matmul(psum[6][0:ETOT, 256:257], maskp[:, :],
                                                ones[:, 0:1], start=True, stop=True),
                     reads=[B_("maskp"), B_("ones")], writes=pw(6, 256, 1))
                S.op("pe", lambda pe: pe.matmul(psum[6][:, 128:128 + ETOT], ltst[:, :],
                                                maskp[:, :], start=True, stop=True),
                     reads=[B_("maskp"), B_("ltst")], writes=pw(6, 128, ETOT))
                S.op("pe", lambda pe: pe.transpose(psum[6][0:1, 384:384 + ETOT],
                                                   roff[0:ETOT, 0:1], ident[0:ETOT, 0:ETOT]),
                     reads=[B_("roff"), B_("ident")], writes=pw(6, 384, ETOT))
                S.op("dve", lambda v: v.tensor_copy(rrow[0:1, :], psum[6][0:1, 384:384 + ETOT]),
                     reads=pw(6, 384, ETOT), writes=[B_("rrow")])
                S.op("pe", lambda pe: pe.matmul(psum[7][:, 32:32 + ETOT], ones[0:1, :],
                                                rrow[0:1, :], start=True, stop=True),
                     reads=[B_("rrow"), B_("ones")], writes=pw(7, 32, ETOT))
                S.op("dve", lambda v: v.tensor_copy(runbc[:, :], psum[7][:, 32:32 + ETOT]),
                     reads=pw(7, 32, ETOT), writes=[B_("runbc")])
                S.op("dve", lambda v: v.tensor_tensor(posg[:, :], psum[6][:, 128:128 + ETOT],
                                                      runbc[:, :], OP.add),
                     reads=pw(6, 128, ETOT) + [B_("runbc")], writes=[B_("posg")])
                S.op("dve", lambda v: v.tensor_tensor(roff[0:ETOT, 0:1], roff[0:ETOT, 0:1],
                                                      psum[6][0:ETOT, 256:257], OP.add),
                     reads=pw(6, 256, 1) + [B_("roff")], writes=[B_("roff")])
                for k in range(2):
                    bE = B_(f"eq{k}")
                    S.op("dve", lambda v, k=k: v.tensor_tensor(
                        dscr[:, :], eqt[k][:, :], iota12bc[:, :], OP.mult),
                        reads=[bE, B_("iota12")], writes=[B_("dscr")])
                    S.op("dve", lambda v, k=k: v.reduce_sum(
                        destf[:, k:k + 1], dscr[:, :], AX.X),
                        reads=[B_("dscr")], writes=[B_("destf")])
                    S.op("dve", lambda v, k=k: v.tensor_tensor(
                        dscr[:, :], eqt[k][:, :], posg[:, :], OP.mult),
                        reads=[bE, B_("posg")], writes=[B_("dscr")])
                    S.op("dve", lambda v, k=k: v.reduce_sum(
                        destf[:, 2 + k:3 + k], dscr[:, :], AX.X),
                        reads=[B_("dscr")], writes=[B_("destf")])
                    S.op("dve", lambda v, k=k: v.tensor_scalar(
                        destf[:, k:k + 1], destf[:, k:k + 1], float(CAP),
                        destf[:, 2 + k:3 + k], OP.mult, OP.add),
                        reads=[B_("destf")], writes=[B_("destf")])
                    S.op("dve", lambda v, k=k: v.tensor_copy(
                        idxu[:, k:k + 1].bitcast(mybir.dt.uint32), destf[:, k:k + 1]),
                        reads=[B_("destf")], writes=[B_("idxu")])
                    S.op("dve", lambda v, k=k, gblk=gblk: v.tensor_scalar(
                        tokw[k][:, 0:1], idxp[:, 0:1], 1.0, float(128 * gblk),
                        OP.mult, OP.add),
                        reads=[B_("idxp")], writes=[B_(f"tokw{k}")])
                    S.op("dve", lambda v, k=k: v.tensor_copy(tokw[k][:, 1:2],
                                                             tcol[:, 3 + k:4 + k]),
                         reads=[bT], writes=[B_(f"tokw{k}")])
                    S.op("gpd", lambda p, k=k, kb=kb: p.indirect_dma_start(
                        eb_x, bass.IndirectOffsetOnAxis(
                            ap=idxu[:, k:k + 1].bitcast(mybir.dt.uint32), axis=0),
                        xn_tm[kb][:, :], None),
                        reads=[B_("idxu"), B_(f"xntm{kb}"), B_("ebz")],
                        writes=[B_("ebsc")])
                    S.op("gpd", lambda p, k=k: p.indirect_dma_start(
                        eb_tw, bass.IndirectOffsetOnAxis(ap=idxu[:, k:k + 1].bitcast(mybir.dt.uint32), axis=0),
                        tokw[k][:, :], None),
                        reads=[B_("idxu"), B_(f"tokw{k}"), B_("ebz")],
                        writes=[B_("ebsc")])
                S.op("dve", lambda v: v.tensor_scalar(
                    eqt[0][:, 8:ETOT], eqt[0][:, 8:ETOT], tcol[:, 3:4], None, OP.mult),
                    reads=[B_("eq0"), bT], writes=[B_("eq0")])
                S.op("dve", lambda v: v.tensor_scalar(eqt[1][:, 8:ETOT], eqt[1][:, 8:ETOT],
                                                      tcol[:, 4:5], None, OP.mult),
                     reads=[B_("eq1"), bT], writes=[B_("eq1")])
                S.op("dve", lambda v: v.tensor_tensor(eqt[0][:, 8:ETOT], eqt[0][:, 8:ETOT],
                                                      eqt[1][:, 8:ETOT], OP.add),
                     reads=[B_("eq0"), B_("eq1")], writes=[B_("eq0")])
                S.op("dve", lambda v: v.reduce_sum(tcol[:, 5:6], eqt[0][:, 8:ETOT], AX.X),
                     reads=[B_("eq0")], writes=[bT])
                S.op("dve", lambda v, kb=kb: v.tensor_scalar(
                    acct[:, :], xn_tm[kb][:, :], tcol[:, 5:6], None, OP.mult),
                    reads=[B_(f"xntm{kb}"), bT], writes=[B_("acct")])
                init_ops.append(S.op("gpd", lambda p, gblk=gblk: p.dma_start(
                    out=acc_h[128 * gblk:128 * gblk + 128, :], in_=acct[:, :]),
                    reads=[B_("acct")], writes=[B_(f"acci{gblk}")]))
        main.reset(m0)
        S.fence()

        m1 = main.mark()
        w1_s = [main.take(DFF) for _ in range(8)]
        w2_s = [main.take(D) for _ in range(16)]
        h_s = [main.take(256) for _ in range(16)]
        xg_tm = [main.take(D) for _ in range(2)]
        xg_fm = [main.take(256) for _ in range(8)]
        twg = [main.take(2) for _ in range(2)]
        otile = main.take(D)
        oidx = main.take(1)

        NQ = CAP // 256
        for e in range(NEXP):
            bW1 = [B_(f"w1s{kc}") for kc in range(8)]
            bW2 = [B_(f"w2s{dc}") for dc in range(16)]
            for kc in range(8):
                S.op("spd", lambda p, e=e, kc=kc: p.dma_start(
                    out=w1_s[kc][:, :], in_=w1t_in[e, 128 * kc:128 * kc + 128, :]),
                    writes=[bW1[kc]])
            for dc in range(16):
                S.op("spd", lambda p, e=e, dc=dc: p.dma_start(
                    out=w2_s[dc][:, :], in_=w2t_in[e, 128 * dc:128 * dc + 128, :]),
                    writes=[bW2[dc]])
            for q in range(NQ):
                r0 = e * CAP + 256 * q
                for b2 in range(2):
                    S.op("spd", lambda p, r0=r0, b2=b2: p.dma_start(
                        out=xg_tm[b2][:, :], in_=eb_x[r0 + 128 * b2:r0 + 128 * b2 + 128, :]),
                        extra=scat_ops, writes=[B_(f"xgtm{b2}")])
                    S.op("spd", lambda p, r0=r0, b2=b2: p.dma_start(
                        out=twg[b2][:, :], in_=eb_tw[r0 + 128 * b2:r0 + 128 * b2 + 128, :]),
                        extra=scat_ops, writes=[B_(f"twg{b2}")])
                    for fc in range(8):
                        bank = 4 + fc % 2
                        S.op("pe", lambda pe, b2=b2, fc=fc, bank=bank: pe.transpose(
                            psum[bank][:, 0:128], xg_tm[b2][:, 128 * fc:128 * fc + 128],
                            ident[:, :]),
                            reads=[B_(f"xgtm{b2}"), B_("ident")], writes=pw(bank, 0, 128))
                        S.op("dve", lambda v, b2=b2, fc=fc, bank=bank: v.tensor_copy(
                            xg_fm[fc][:, 128 * b2:128 * b2 + 128], psum[bank][:, 0:128]),
                            reads=pw(bank, 0, 128), writes=[B_(f"xgfm{fc}")])
                for dc in range(16):
                    pn = dc % 2
                    for kc in range(8):
                        S.op("pe", lambda pe, dc=dc, kc=kc, pn=pn: pe.matmul(
                            psum[pn][:, 0:256], w1_s[kc][:, 128 * dc:128 * dc + 128],
                            xg_fm[kc][:, :], start=(kc == 0), stop=(kc == 7)),
                            reads=[B_(f"xgfm{kc}"), bW1[kc]], writes=pw(pn, 0, 256))
                    S.op("act", lambda a, dc=dc, pn=pn: a.activation(
                        out=h_s[dc][:, :], in_=psum[pn][:, 0:256], func=AF.Silu),
                        reads=pw(pn, 0, 256), writes=[B_(f"h{dc}")])
                for b2 in range(2):
                    for dh in range(2):
                        pn = 2 + dh
                        for dc in range(16):
                            S.op("pe", lambda pe, b2=b2, dh=dh, dc=dc, pn=pn: pe.matmul(
                                psum[pn][:, 0:512],
                                h_s[dc][:, 128 * b2:128 * b2 + 128],
                                w2_s[dc][:, 512 * dh:512 * dh + 512],
                                start=(dc == 0), stop=(dc == 15)),
                                reads=[B_(f"h{dc}"), bW2[dc]], writes=pw(pn, 0, 512))
                        S.op("dve", lambda v, b2=b2, dh=dh, pn=pn: v.tensor_scalar(
                            otile[:, 512 * dh:512 * dh + 512], psum[pn][:, 0:512],
                            twg[b2][:, 1:2], None, OP.mult),
                            reads=pw(pn, 0, 512) + [B_(f"twg{b2}")], writes=[B_("otile")])
                    S.op("dve", lambda v, b2=b2: v.tensor_copy(
                        oidx[:, 0:1].bitcast(mybir.dt.uint32), twg[b2][:, 0:1]),
                        reads=[B_(f"twg{b2}")], writes=[B_("oidx")])
                    S.op("gpd", lambda p: p.indirect_dma_start(
                        acc_h, bass.IndirectOffsetOnAxis(ap=oidx[:, 0:1].bitcast(mybir.dt.uint32), axis=0),
                        otile[:, :], None, compute_op=OP.add),
                        reads=[B_("oidx"), B_("otile")], extra=init_ops,
                        writes=[B_("accsc")])
        main.reset(m1)
        S.fence()

        m2 = main.mark()
        gtile = main.take(D)
        for gblk in range(B * tloc // 128):
            S.op("spd", lambda p, gblk=gblk: p.dma_start(
                out=gtile[:, :], in_=acc_h[128 * gblk:128 * gblk + 128, :]),
                reads=[B_("accsc")], writes=[B_("gtile")])
            S.op("act", lambda a: a.activation(out=gtile[:, :], in_=gtile[:, :],
                                               func=AF.Silu),
                 reads=[B_("gtile")], writes=[B_("gtile")])
            S.op("gpd", lambda p, gblk=gblk: p.dma_start(
                out=sc_g[128 * gblk:128 * gblk + 128, :], in_=gtile[:, :]),
                reads=[B_("gtile")], writes=[B_(f"scgb{gblk}")])
        main.reset(m2)

    def phase3():
        m0 = main.mark()
        outpw_s = {d: [main.take(DH) for _ in range(8)] for d in "fb"}
        opw_s = [main.take(D) for _ in range(8)]
        h0 = [main.take(DIN) for _ in range(4)]
        hrun = main.take(DIN)
        sblk = main.take(DIN)
        gblk = main.take(NH)
        scr3 = main.take(DIN)
        c3 = main.take(8)
        hle = {dd: main.take(DIN) for dd in "fb"}
        cch = {dd: main.take(Q) for dd in "fb"}
        yld = {dd: main.take(DIN) for dd in "fb"}
        ycb = {dd: main.take(DIN) for dd in "fb"}
        ynm = {dd: main.take(DIN) for dd in "fb"}
        yfm = {dd: main.take(DIN) for dd in "fb"}
        zld = main.take(DIN)
        ecol = main.take(NH)
        erow3 = main.take(NH)
        cat = main.take(D)
        sfm = main.take(D)
        gld = main.take(D)
        xnld = main.take(D)
        xld = main.take(D)
        otm = main.take(D)
        pfm = main.take(D)

        for d in "fb":
            for kc in range(8):
                S.op("spd", lambda p, d=d, kc=kc: p.dma_start(
                    out=outpw_s[d][kc][:, :], in_=outpw_in[d][128 * kc:128 * kc + 128, :]),
                    writes=[B_(f"outpw{d}{kc}")])
        for kc in range(8):
            S.op("spd", lambda p, kc=kc: p.dma_start(
                out=opw_s[kc][:, :], in_=opw_in[128 * kc:128 * kc + 128, :]),
                writes=[B_(f"opw{kc}")])

        for bd in range(4):
            d = "f" if bd % 2 == 0 else "b"
            bH0 = B_(f"h0_{bd}")
            bHr = B_("hrun")
            S.op("dve", lambda v, bd=bd: v.memset(h0[bd][:, :], 0.0), writes=[bH0])
            S.op("dve", lambda v: v.memset(hrun[:, :], 0.0), writes=[bHr])
            order = list(range(ncores_run)) if d == "f" else list(range(ncores_run - 1, -1, -1))
            for j in order:
                S.op("spd", lambda p, j=j, bd=bd: p.dma_start(
                    out=sblk[:, :], in_=ag_out[128 * j:128 * j + 128, DIN * bd:DIN * bd + DIN]),
                    reads=[B_("agout")], writes=[B_("sblk")])
                S.op("spd", lambda p, j=j, bd=bd: p.dma_start(
                    out=gblk[:, :],
                    in_=ag_out[128 * j:128 * j + 128, 4 * DIN + 16 * bd:4 * DIN + 16 * bd + 16]),
                    reads=[B_("agout")], writes=[B_("gblk")])
                S.op("dve", lambda v: v.tensor_tensor(
                    hrun.rearrange("p (h q) -> p h q", h=NH),
                    hrun.rearrange("p (h q) -> p h q", h=NH),
                    gblk.unsqueeze(2).broadcast_to((128, NH, HDIM)), OP.mult),
                    reads=[B_("gblk")], writes=[bHr])
                S.op("dve", lambda v: v.tensor_tensor(hrun[:, :], hrun[:, :],
                                                      sblk[:, :], OP.add),
                     reads=[B_("sblk")], writes=[bHr])
                S.op("dve", lambda v, j=j, bd=bd, d=d: v.scalar_tensor_tensor(
                    h0[bd][:, :], hrun[:, :], msnap_s[d][:, j:j + 1], h0[bd][:, :],
                    OP.mult, OP.add),
                    reads=[bHr, B_("msnap" + d)], writes=[bH0])

        def member(b, d, ch):
            bd = 2 * b + (0 if d == "f" else 1)
            mbk = 0 if d == "f" else 1
            bHle = B_("hle" + d)
            S.op("spd", lambda p: p.dma_start(out=hle[d][:, :], in_=sc_hl[bd, ch, :, :]),
                 reads=[B_(f"schl{bd}_{ch}")], writes=[bHle])
            if ch == 0:
                ec = ones16
                bEc = B_("ones16")
            else:
                ec = ecol
                bEc = B_("ecol")
                S.op("pe", lambda pe, bd=bd, ch=ch: pe.transpose(
                    psum[6 + mbk][0:1, 32:32 + NH], ebt[bd][0:NH, ch - 1:ch],
                    ident[0:NH, 0:NH]),
                    reads=[B_(f"ebt{bd}"), B_("ident")], writes=pw(6 + mbk, 32, NH))
                S.op("dve", lambda v: v.tensor_copy(erow3[0:1, 0:NH],
                                                    psum[6 + mbk][0:1, 32:32 + NH]),
                     reads=pw(6 + mbk, 32, NH), writes=[B_("erow3")])
                S.op("pe", lambda pe: pe.matmul(
                    psum[6 + mbk][:, 0:NH], ones[0:1, :], erow3[0:1, 0:NH],
                    start=True, stop=True),
                    reads=[B_("erow3"), B_("ones")], writes=pw(6 + mbk, 0, NH))
                S.op("dve", lambda v: v.tensor_copy(ecol[:, :], psum[6 + mbk][:, 0:NH]),
                     reads=pw(6 + mbk, 0, NH), writes=[bEc])
            S.op("dve", lambda v, bd=bd, ec=ec: v.tensor_tensor(
                scr3.rearrange("p (h q) -> p h q", h=NH),
                h0[bd].rearrange("p (h q) -> p h q", h=NH),
                ec.unsqueeze(2).broadcast_to((128, NH, HDIM)), OP.mult),
                reads=[B_(f"h0_{bd}"), bEc], writes=[B_("scr3")])
            S.op("dve", lambda v: v.tensor_tensor(hle[d][:, :], hle[d][:, :],
                                                  scr3[:, :], OP.add),
                 reads=[B_("scr3")], writes=[bHle])
            S.op("spd", lambda p: p.dma_start(out=cch[d][:, :],
                                              in_=sc_c[bd, :, Q * ch:Q * ch + Q]),
                 reads=[B_(f"scc{bd}")], writes=[B_("cch" + d)])
            for half in range(2):
                S.op("pe", lambda pe, half=half, mbk=mbk: pe.matmul(
                    psum[2 * half + mbk][:, 0:512], cch[d][:, :],
                    hle[d][:, 512 * half:512 * half + 512], start=True, stop=True),
                    reads=[B_("cch" + d), bHle], writes=pw(2 * half + mbk, 0, 512))
            S.op("spd", lambda p: p.dma_start(out=yld[d][:, :], in_=sc_y[bd, ch, :, :]),
                 reads=[B_(f"scy{bd}_{ch}")], writes=[B_("yld" + d)])
            for half in range(2):
                S.op("dve", lambda v, half=half, bd=bd, ch=ch, mbk=mbk: v.tensor_tensor(
                    ycb[d][:, 512 * half:512 * half + 512]
                        .rearrange("p (h q) -> p h q", h=8),
                    psum[2 * half + mbk][:, 0:512].rearrange("p (h q) -> p h q", h=8),
                    edel[bd][ch][:, 8 * half:8 * half + 8]
                        .unsqueeze(2).broadcast_to((128, 8, HDIM)), OP.mult),
                    reads=pw(2 * half + mbk, 0, 512) + [B_(f"edel{bd}_{ch}")],
                    writes=[B_("ycb" + d)])
            S.op("dve", lambda v: v.tensor_tensor(ycb[d][:, :], ycb[d][:, :],
                                                  yld[d][:, :], OP.add),
                 reads=[B_("yld" + d)], writes=[B_("ycb" + d)])
            for fc in range(8):
                S.op("spd", lambda p, fc=fc: p.dma_start(
                    out=zld[:, 128 * fc:128 * fc + 128],
                    in_=sc_z[bd, fc, :, Q * ch:Q * ch + Q]),
                    reads=[B_(f"scz{bd}_{fc}")], writes=[B_(f"zld{fc}")])
            for fc in range(8):
                bank = 4 + mbk
                S.op("pe", lambda pe, fc=fc, bank=bank: pe.transpose(
                    psum[bank][:, 0:128], zld[:, 128 * fc:128 * fc + 128], ident[:, :]),
                    reads=[B_(f"zld{fc}"), B_("ident")], writes=pw(bank, 0, 128))
                S.op("dve", lambda v, fc=fc, bank=bank: v.tensor_tensor(
                    ycb[d][:, 128 * fc:128 * fc + 128], psum[bank][:, 0:128],
                    ycb[d][:, 128 * fc:128 * fc + 128], OP.mult),
                    reads=pw(bank, 0, 128), writes=[B_("ycb" + d)])
            S.op("act", lambda a: a.activation(out=scr3[:, :], in_=ycb[d][:, :],
                                               func=AF.Square, accum_out=c3[:, 0:1]),
                 reads=[B_("ycb" + d)], writes=[B_("scr3"), B_("c3")])
            S.op("act", lambda a: a.activation(out=c3[:, 1:2], in_=c3[:, 0:1],
                                               func=AF.Sqrt, scale=1.0 / DIN,
                                               bias=epscol[:, :]),
                 reads=[B_("c3"), B_("epscol")], writes=[B_("c3")])
            S.op("dve", lambda v: v.reciprocal(c3[:, 1:2], c3[:, 1:2]),
                 reads=[B_("c3")], writes=[B_("c3")])
            S.op("dve", lambda v: v.scalar_tensor_tensor(
                ynm[d][:, :], ycb[d][:, :], c3[:, 1:2], gnw_bc[d][:, :],
                OP.mult, OP.mult),
                reads=[B_("ycb" + d), B_("c3"), B_("gnw" + d)], writes=[B_("ynm" + d)])
            for fc in range(8):
                bank = 4 + mbk
                S.op("pe", lambda pe, fc=fc, bank=bank: pe.transpose(
                    psum[bank][:, 128:256], ynm[d][:, 128 * fc:128 * fc + 128],
                    ident[:, :]),
                    reads=[B_("ynm" + d), B_("ident")], writes=pw(bank, 128, 128))
                dst = yfm[d][:, 128 * fc:128 * fc + 128]
                if d == "b":
                    dst = dst[:, ::-1]
                S.op("dve", lambda v, dst=dst, bank=bank: v.tensor_copy(
                    dst, psum[bank][:, 128:256]),
                    reads=pw(bank, 128, 128), writes=[B_("yfm" + d)])
            coff = 0 if d == "f" else 512
            for oc in range(4):
                pn = 2 + mbk
                for fc in range(8):
                    S.op("pe", lambda pe, fc=fc, oc=oc, pn=pn, d=d: pe.matmul(
                        psum[pn][:, 0:128],
                        outpw_s[d][fc][:, 128 * oc:128 * oc + 128],
                        yfm[d][:, 128 * fc:128 * fc + 128],
                        start=(fc == 0), stop=(fc == 7)),
                        reads=[B_("yfm" + d), B_(f"outpw{d}{fc}")],
                        writes=pw(pn, 0, 128))
                S.op("dve", lambda v, oc=oc, pn=pn, coff=coff: v.tensor_copy(
                    cat[:, coff + 128 * oc:coff + 128 * oc + 128],
                    psum[pn][:, 0:128]),
                    reads=pw(pn, 0, 128), writes=[B_("cat")])

        for b in range(B):
            for jf in range(NCH):
                member(b, "f", jf)
                member(b, "b", NCH - 1 - jf)
                g0 = b * tloc + Q * jf
                for fc in range(8):
                    S.op("spd", lambda p, fc=fc, g0=g0: p.dma_start(
                        out=xnld[:, 128 * fc:128 * fc + 128],
                        in_=sc_xn[fc, :, g0:g0 + Q]),
                        reads=[B_(f"scxn{fc}")], writes=[B_(f"xnld{fc}")])
                S.op("spd", lambda p, g0=g0: p.dma_start(
                    out=gld[:, :], in_=sc_g[g0:g0 + Q, :]),
                    reads=[B_(f"scgb{g0 // 128}")], writes=[B_("gld")])
                S.op("spd", lambda p, b=b, jf=jf: p.dma_start(
                    out=xld[:, :], in_=x_in[b, 3 + Q * jf:3 + Q * jf + Q, :]),
                    writes=[B_("xld")])
                for oc in range(8):
                    pn = oc % 2
                    for kc in range(8):
                        S.op("pe", lambda pe, oc=oc, kc=kc, pn=pn: pe.matmul(
                            psum[pn][:, 0:128], opw_s[kc][:, 128 * oc:128 * oc + 128],
                            cat[:, 128 * kc:128 * kc + 128], start=(kc == 0),
                            stop=(kc == 7)),
                            reads=[B_("cat"), B_(f"opw{kc}")], writes=pw(pn, 0, 128))
                    S.op("dve", lambda v, oc=oc, pn=pn: v.scalar_tensor_tensor(
                        sfm[:, 128 * oc:128 * oc + 128], psum[pn][:, 0:128],
                        opb_c[:, oc:oc + 1], xnld[:, 128 * oc:128 * oc + 128],
                        OP.add, OP.add),
                        reads=pw(pn, 0, 128) + [B_("opb"), B_(f"xnld{oc}")],
                        writes=[B_("sfm")])
                for fc in range(8):
                    bank = 4 + fc % 2
                    S.op("pe", lambda pe, fc=fc, bank=bank: pe.transpose(
                        psum[bank][:, 256:384], sfm[:, 128 * fc:128 * fc + 128],
                        ident[:, :]),
                        reads=[B_("sfm"), B_("ident")], writes=pw(bank, 256, 128))
                    S.op("dve", lambda v, fc=fc, bank=bank: v.tensor_tensor(
                        pfm[:, 128 * fc:128 * fc + 128], psum[bank][:, 256:384],
                        gld[:, 128 * fc:128 * fc + 128], OP.mult),
                        reads=pw(bank, 256, 128) + [B_("gld")], writes=[B_("pfm")])
                S.op("dve", lambda v: v.tensor_tensor(otm[:, :], pfm[:, :],
                                                      xld[:, :], OP.add),
                     reads=[B_("pfm"), B_("xld")], writes=[B_("otm")])
                oi = S.op("gpd", lambda p, b=b, jf=jf: p.dma_start(
                    out=out_ext[b, Q * jf:Q * jf + Q, :], in_=otm[:, :]),
                    reads=[B_("otm")], writes=[B_(f"outdma{b}_{jf}")])
                out_dmas.append(oi)
                if _dbgck and _dbgck.startswith("m,") and \
                        [b, jf] == [int(v) for v in _dbgck.split(",")[1:]]:
                    for di, (bn, ap) in enumerate([
                            ("cat", cat[:, :]), ("sfm", sfm[:, :]),
                            ("gld", gld[:, :]), ("xld", xld[:, :]),
                            ("xnld7", xnld[:, :]), ("otm", otm[:, :]),
                            ("ycbf", ycb["f"][:, :]), ("ycbb", ycb["b"][:, :]),
                            ("ynmf", ynm["f"][:, :]), ("ynmb", ynm["b"][:, :]),
                            ("yfmf", yfm["f"][:, :]), ("yfmb", yfm["b"][:, :]),
                            ("hlef", hle["f"][:, :]), ("yldf", yld["f"][:, :]),
                            ("zld", zld[:, :]), ("c3", c3[:, :])]):
                        S.op("gpd", lambda p, di=di, ap=ap: p.dma_start(
                            out=dbg_t[di, 0:ap.shape[0], 0:ap.shape[-1]], in_=ap),
                            reads=[B_(bn)], writes=[B_(f"dbgm{di}")])
        main.reset(m0)

    out_dmas = []
    setup_consts()
    S.fence()
    if "1" in phases:
        for b in range(B):
            phase1_pass(b, "f")
        for b in range(B):
            phase1_pass(b, "b")
        S.fence()
    S.op("gpd", lambda p: p.dma_start(out=ag_in[:, 4 * DIN:4 * DIN + 64],
                                      in_=gam_ag[:, :]),
         reads=[B_("gamag")], writes=[B_("agin")])
    if ncores_run > 1:
        S.op("gp", lambda g: g.collective_compute(
            "AllGather", OP.bypass, replica_groups=[list(range(ncores_run))],
            ins=[ag_in.opt()], outs=[ag_out.opt()]),
            reads=[B_("agin")], writes=[B_("agout")])
    else:
        S.op("gpd", lambda p: p.dma_start(out=ag_out[:, :], in_=ag_in[:, :]),
             reads=[B_("agin")], writes=[B_("agout")])
    if "2" in phases:
        phase2()
    S.fence()
    if "3" in phases:
        phase3()
    S.op("gp", lambda g: g.nop(), extra=list(out_dmas))

    sems = {}
    dsems = {}
    semcms = []
    for k in Sched.CKEYS:
        cm = nc.semaphore(f"sem_{k}")
        sems[k] = cm.__enter__()
        semcms.append(cm)
    for k in Sched.DKEYS:
        dsems[k] = []
        for s in range(Sched.NDSEM):
            cm = nc.semaphore(f"sem_{k}{s}")
            dsems[k].append(cm.__enter__())
            semcms.append(cm)
    with nc.Block() as block:
        S.emit(block, sems, dsems)
    for cm in reversed(semcms):
        cm.__exit__(None, None, None)
    return nc


_NC_CACHE = {}


def _get_nc(tloc, ncores):
    key = (tloc, ncores)
    if key not in _NC_CACHE:
        _NC_CACHE[key] = build_nc(tloc, ncores)
    return _NC_CACHE[key]


def make_in_maps(inputs, tloc, ncores):
    f32 = lambda a: np.ascontiguousarray(np.asarray(a), dtype=np.float32)
    x = f32(inputs["L0_attn_x"])
    T = x.shape[1]
    assert T == tloc * ncores
    base = {
        "inwT_f": f32(inputs["L0_attn_f_in_w"].T),
        "inwT_b": f32(inputs["L0_attn_b_in_w"].T),
        "convw_f": f32(inputs["L0_attn_f_conv_w"]),
        "convw_b": f32(inputs["L0_attn_b_conv_w"]),
        "convb_f": f32(np.reshape(inputs["L0_attn_f_conv_b"], (10, 128)).T),
        "convb_b": f32(np.reshape(inputs["L0_attn_b_conv_b"], (10, 128)).T),
        "dtb_f": f32(inputs["L0_attn_f_dt_bias"]).reshape(NH, 1),
        "dtb_b": f32(inputs["L0_attn_b_dt_bias"]).reshape(NH, 1),
        "alog_f": f32(inputs["L0_attn_f_A_log"]).reshape(NH, 1),
        "alog_b": f32(inputs["L0_attn_b_A_log"]).reshape(NH, 1),
        "dd_f": f32(inputs["L0_attn_f_D"]).reshape(1, NH),
        "dd_b": f32(inputs["L0_attn_b_D"]).reshape(1, NH),
        "gnw_f": f32(inputs["L0_attn_f_gnorm_w"]).reshape(1, DIN),
        "gnw_b": f32(inputs["L0_attn_b_gnorm_w"]).reshape(1, DIN),
        "outpwT_f": f32(inputs["L0_attn_f_outp_w"].T),
        "outpwT_b": f32(inputs["L0_attn_b_outp_w"].T),
        "normw": f32(inputs["L0_attn_norm_w"]).reshape(1, D),
        "opwT": f32(inputs["L0_attn_op_w"].T),
        "opb": f32(np.reshape(inputs["L0_attn_op_b"], (8, 128)).T),
        "gatewT": f32(inputs["L0_ffn_gate_w"].T),
        "ebias": f32(inputs["L0_ffn_expert_bias"]).reshape(ETOT, 1),
        "w1T": f32(np.transpose(inputs["L0_ffn_w1"], (0, 2, 1))),
        "w2T": f32(np.transpose(inputs["L0_ffn_w2"], (0, 2, 1))),
    }
    in_maps = []
    for c in range(ncores):
        xs = np.zeros((B, tloc + 6, D), np.float32)
        s0 = c * tloc
        s1 = s0 + tloc
        xs[:, 3:3 + tloc, :] = x[:, s0:s1, :]
        if s0 > 0:
            xs[:, 0:3, :] = x[:, s0 - 3:s0, :]
        if s1 < T:
            xs[:, 3 + tloc:, :] = x[:, s1:s1 + 3, :]
        mf = np.zeros((128, 8), np.float32)
        mb = np.zeros((128, 8), np.float32)
        if c > 0:
            mf[:, c - 1] = 1.0
        if c < ncores - 1:
            mb[:, c + 1] = 1.0
        m = dict(base)
        m["x"] = xs
        m["msnap_f"] = mf
        m["msnap_b"] = mb
        in_maps.append(m)
    return in_maps


def kernel(**inputs):
    ncores = NCORES
    T = np.asarray(inputs["L0_attn_x"]).shape[1]
    tloc = T // ncores
    nc = _get_nc(tloc, ncores)
    in_maps = make_in_maps(inputs, tloc, ncores)
    res = run_bass_kernel_spmd(nc, in_maps, core_ids=list(range(ncores)))
    return np.concatenate([r["out"] for r in res.results], axis=1)


# version 15
# speedup vs baseline: 1.0112x; 1.0112x over previous
import numpy as np
import concourse.bass as bass
import concourse.mybir as mybir
from concourse.bass_utils import run_bass_kernel_spmd
from concourse.alu_op_type import AluOpType as OP

F32 = mybir.dt.float32
AF = mybir.ActivationFunctionType
AX = mybir.AxisListType

B = 2
D = 1024
DH = 512
DIN = 1024
NH = 16
HDIM = 64
DST = 128
DCONV = 4
CONV_DIM = 1280
DIP = 2320
DFF = 2048
NEXP = 8
ETOT = 12
EPS = 1e-5
Q = 128
NCORES = 8
CAP = 1024


class Buf:
    __slots__ = ("writer", "readers", "name")

    def __init__(self, name=""):
        self.writer = None
        self.readers = set()
        self.name = name


class Sched:
    CKEYS = ("pe", "act", "dve", "gp")
    DKEYS = ("spd", "gpd", "acd")

    def __init__(self):
        self.ops = []
        self.fence_deps = []

    def op(self, key, fn, reads=(), writes=(), extra=()):
        dma = key in self.DKEYS
        psr = [b for b in reads if b is not None and b.name.startswith("ps")]
        if psr:
            reads = [b for b in reads if b is None or not b.name.startswith("ps")]
            writes = list(writes) + psr
        deps = set(self.fence_deps)
        for b in reads:
            if b is not None and b.writer is not None:
                deps.add(b.writer)
        for b in writes:
            if b is not None:
                deps.update(b.readers)
                if b.writer is not None:
                    deps.add(b.writer)
        for dd in extra:
            if dd is not None:
                deps.add(dd)
        i = len(self.ops)
        deps.discard(i)
        self.ops.append({"key": key, "fn": fn, "deps": sorted(deps), "dma": dma})
        for b in reads:
            if b is not None:
                b.readers.add(i)
        for b in writes:
            if b is not None:
                b.writer = i
                b.readers = set()
        return i

    def fence(self):
        last = {}
        for i, o in enumerate(self.ops):
            last[o["key"]] = i
        self.fence_deps = sorted(last.values())

    NDSEM = 12

    def emit(self, block, sems, dsems):
        import os as _os
        _mo = int(_os.environ.get("BASS_MAXOPS", "0"))
        ops = self.ops[:_mo] if _mo > 0 else self.ops
        needed = set()
        for o in ops:
            needed.update(o["deps"])
        incval = {}
        slot_of = {}
        ord_of = {}
        cnt = {k: 0 for k in self.CKEYS}
        dcnt = {k: 0 for k in self.DKEYS}
        for i, o in enumerate(ops):
            if o["dma"]:
                n = dcnt[o["key"]]
                dcnt[o["key"]] += 1
                slot_of[i] = n % self.NDSEM
                ord_of[i] = n // self.NDSEM + 1
            elif i in needed:
                cnt[o["key"]] += 1
                incval[i] = cnt[o["key"]]
        key2eng = {"pe": "tensor", "act": "scalar", "dve": "vector",
                   "gp": "gpsimd", "spd": "sync", "gpd": "gpsimd",
                   "acd": "scalar"}

        epos = {}
        ecnt = {}
        for i, o in enumerate(ops):
            if not o["dma"]:
                e = key2eng[o["key"]]
                epos[i] = ecnt.get(e, 0)
                ecnt[e] = epos[i] + 1

        def emit_for(eng_name, proxy):
            seen = {k: 0 for k in cnt}
            dseen = {(k, s): 0 for k in self.DKEYS for s in range(self.NDSEM)}
            for i, o in enumerate(ops):
                if key2eng[o["key"]] != eng_name:
                    continue
                if o["dma"]:
                    k, s, n = o["key"], slot_of[i], ord_of[i]
                    if n > 1 and dseen[(k, s)] < 16 * (n - 1):
                        proxy.wait_ge(dsems[k][s], 16 * (n - 1))
                        dseen[(k, s)] = 16 * (n - 1)
                for dd in o["deps"]:
                    do = ops[dd]
                    dk = do["key"]
                    if do["dma"]:
                        ks = (dk, slot_of[dd])
                        v = 16 * ord_of[dd]
                        if v > dseen[ks]:
                            proxy.wait_ge(dsems[dk][slot_of[dd]], v)
                            dseen[ks] = v
                    else:
                        if key2eng[dk] == eng_name:
                            if o["dma"] or eng_name == "gpsimd":
                                pass
                            elif eng_name == "tensor":
                                continue
                            elif epos[i] - epos[dd] > 1:
                                continue
                            v = incval[dd]
                            if v > seen[dk]:
                                proxy.wait_ge(sems[dk], v)
                                seen[dk] = v
                            continue
                        v = incval[dd]
                        if v > seen[dk]:
                            proxy.wait_ge(sems[dk], v)
                            seen[dk] = v
                h = o["fn"](proxy)
                if o["dma"]:
                    h.then_inc(dsems[o["key"]][slot_of[i]], 16)
                elif i in incval:
                    h.then_inc(sems[o["key"]], 1)

        @block.tensor
        def _(p):
            emit_for("tensor", p)

        @block.scalar
        def _(p):
            emit_for("scalar", p)

        @block.vector
        def _(p):
            emit_for("vector", p)

        @block.gpsimd
        def _(p):
            emit_for("gpsimd", p)

        @block.sync
        def _(p):
            emit_for("sync", p)


class Arena:
    def __init__(self, ap_full, ncols):
        self.t = ap_full
        self.ncols = ncols
        self.off = 0

    def take(self, cols):
        cols = int(cols)
        assert self.off + cols <= self.ncols, \
            f"arena overflow: {self.off}+{cols} > {self.ncols}"
        ap = self.t[:, self.off:self.off + cols]
        self.off += cols
        return ap

    def mark(self):
        return self.off

    def reset(self, mark):
        self.off = mark


def build_nc(tloc, ncores_run, phases="123"):
    TT = 256
    NTILE = tloc // TT
    NCH = tloc // Q
    CPT = TT // Q
    TT2 = 256
    NT2 = (B * tloc) // TT2

    nc = bass.Bass("TRN2", target_bir_lowering=False, detect_race_conditions=False)

    def din(name, shape):
        return nc.dram_tensor(name, list(shape), F32, kind="ExternalInput").ap()

    x_in = din("x", (B, tloc + 6, D))
    inw_in = {"f": din("inwT_f", (DH, DIP)), "b": din("inwT_b", (DH, DIP))}
    convw_in = {"f": din("convw_f", (CONV_DIM, DCONV)), "b": din("convw_b", (CONV_DIM, DCONV))}
    convb_in = {"f": din("convb_f", (128, 10)), "b": din("convb_b", (128, 10))}
    dtb_in = {"f": din("dtb_f", (NH, 1)), "b": din("dtb_b", (NH, 1))}
    alog_in = {"f": din("alog_f", (NH, 1)), "b": din("alog_b", (NH, 1))}
    dd_in = {"f": din("dd_f", (1, NH)), "b": din("dd_b", (1, NH))}
    gnw_in = {"f": din("gnw_f", (1, DIN)), "b": din("gnw_b", (1, DIN))}
    outpw_in = {"f": din("outpwT_f", (DIN, DH)), "b": din("outpwT_b", (DIN, DH))}
    normw_in = din("normw", (1, D))
    opw_in = din("opwT", (D, D))
    opb_in = din("opb", (128, 8))
    gatew_in = din("gatewT", (D, ETOT))
    ebias_in = din("ebias", (ETOT, 1))
    w1t_in = din("w1T", (NEXP, D, DFF))
    w2t_in = din("w2T", (NEXP, DFF, D))
    msnap_in = {"f": din("msnap_f", (128, 8)), "b": din("msnap_b", (128, 8))}

    out_ext = nc.dram_tensor("out", [B, tloc, D], F32, kind="ExternalOutput").ap()

    import os as _os
    _dbg = {"kind": "ExternalOutput"} if _os.environ.get("DEBUG_SC") else {}
    sc_xn = nc.dram_tensor("sc_xn", [8, 128, B * tloc], F32, **_dbg).ap()
    sc_z = nc.dram_tensor("sc_z", [4, 8, 128, tloc], F32, **_dbg).ap()
    sc_y = nc.dram_tensor("sc_y", [4, NCH, 128, DIN], F32, **_dbg).ap()
    sc_hl = nc.dram_tensor("sc_hl", [4, NCH, 128, DIN], F32, **_dbg).ap()
    sc_c = nc.dram_tensor("sc_c", [4, 128, tloc], F32, **_dbg).ap()
    sc_g = nc.dram_tensor("sc_g", [B * tloc, D], F32, **_dbg).ap()
    _dbgck = _os.environ.get("DEBUG_CHUNK")
    dbg_t = (nc.dram_tensor("dbg", [16, 128, 2048], F32, kind="ExternalOutput").ap()
             if _dbgck else None)
    eb_x = nc.dram_tensor("eb_x", [ETOT * CAP, D], F32).ap()
    eb_tw = nc.dram_tensor("eb_tw", [ETOT * CAP, 2], F32).ap()
    acc_h = nc.dram_tensor("acc_h", [B * tloc + ETOT * CAP, D], F32).ap()
    AGC = 4 * DIN + 64
    ag_in = nc.dram_tensor("ag_in", [128, AGC], F32).ap()
    ag_out = nc.dram_tensor("ag_out", [ncores_run * 128, AGC], F32).ap()

    rem = nc.sbuf_bytes_remaining // 4
    pers_cols = 7000 + 4 * NCH * NH
    main_cols = min(rem - pers_cols - 128, 47000)
    pers = Arena(nc.alloc_sbuf_tensor("pers", [128, pers_cols], F32).ap(), pers_cols)
    main = Arena(nc.alloc_sbuf_tensor("main", [128, main_cols], F32).ap(), main_cols)

    psum = [nc.alloc_psum_tensor(f"ps{i}", [128, 512], F32).ap() for i in range(8)]
    pb = [Buf(f"ps{i}") for i in range(8)]

    def pw(bank, col0, ncols):
        return [pb[bank]]

    S = Sched()
    bufs = {}
    bufs_zero_ops = []

    def B_(name):
        if name not in bufs:
            bufs[name] = Buf(name)
        return bufs[name]

    ones = pers.take(128)
    ident = pers.take(128)
    lt = pers.take(128)
    idxf = pers.take(128)
    idxp = pers.take(128)
    normw_bc = pers.take(D)
    gnw_bc = {"f": pers.take(DIN), "b": pers.take(DIN)}
    dcol = {"f": pers.take(NH), "b": pers.take(NH)}
    acol = {d: pers.take(1) for d in "fb"}
    dtbias = {d: pers.take(1) for d in "fb"}
    convw = {d: pers.take(4 * 10) for d in "fb"}
    convb = {d: pers.take(10) for d in "fb"}
    opb_c = pers.take(8)
    ebias_c = pers.take(1)
    ones16 = pers.take(NH)
    ltst = pers.take(128)
    iota12bc = pers.take(ETOT)
    zt = pers.take(D)
    tid_t = pers.take(192)
    epscol = pers.take(1)
    onecol = pers.take(1)
    gatew_s = pers.take(ETOT * 8)
    msnap_s = {d: pers.take(8) for d in "fb"}
    edel = {bd: [pers.take(NH) for _ in range(NCH)] for bd in range(4)}
    ebt = {bd: pers.take(NH) for bd in range(4)}
    bounds = {bd: pers.take(NCH) for bd in range(4)}
    gam_ag = pers.take(64)

    def setup_consts():
        S.op("dve", lambda v: v.memset(ones[:, :], 1.0), writes=[B_("ones")])
        S.op("dve", lambda v: v.memset(ones16[:, :], 1.0), writes=[B_("ones16")])
        S.op("dve", lambda v: v.memset(epscol[:, :], EPS), writes=[B_("epscol")])
        S.op("dve", lambda v: v.memset(onecol[:, :], 1.0), writes=[B_("onecol")])
        S.op("gp", lambda g: g.iota(idxf[:, :], [[1, 128]], base=0, channel_multiplier=0,
                                    allow_small_or_imprecise_dtypes=True),
             writes=[B_("idxf")])
        S.op("gp", lambda g: g.iota(idxp[:, :], [[0, 128]], base=0, channel_multiplier=1,
                                    allow_small_or_imprecise_dtypes=True),
             writes=[B_("idxp")])
        S.op("dve", lambda v: v.tensor_tensor(ident[:, :], idxf[:, :], idxp[:, :], OP.is_equal),
             reads=[B_("idxf"), B_("idxp")], writes=[B_("ident")])
        S.op("dve", lambda v: v.tensor_tensor(lt[:, :], idxf[:, :], idxp[:, :], OP.is_ge),
             reads=[B_("idxf"), B_("idxp")], writes=[B_("lt")])
        S.op("dve", lambda v: v.tensor_tensor(ltst[:, :], idxf[:, :], idxp[:, :], OP.is_gt),
             reads=[B_("idxf"), B_("idxp")], writes=[B_("ltst")])
        S.op("dve", lambda v: v.memset(zt[:, :], 0.0), writes=[B_("zt")])
        S.op("dve", lambda v: v.memset(tid_t[:, :], 0.0), writes=[B_("tid")])
        S.op("gp", lambda g: g.iota(tid_t[:, 0:192:2], [[128, 96]],
                                    base=B * tloc, channel_multiplier=1,
                                    allow_small_or_imprecise_dtypes=True),
             reads=[B_("tid")], writes=[B_("tid")])
        S.op("pe", lambda pe: pe.matmul(psum[7][:, 64:64 + ETOT], ones[0:1, :],
                                        idxf[0:1, 0:ETOT], start=True, stop=True),
             reads=[B_("idxf"), B_("ones")], writes=pw(7, 64, ETOT))
        S.op("dve", lambda v: v.tensor_copy(iota12bc[:, :], psum[7][:, 64:64 + ETOT]),
             reads=pw(7, 64, ETOT), writes=[B_("iota12")])
        zero_ops = []
        for blk in range(8 * CAP // 128):
            zero_ops.append(S.op("gpd", lambda p, blk=blk: p.dma_start(
                out=eb_x[128 * blk:128 * blk + 128, :], in_=zt[:, :]),
                reads=[B_("zt")], writes=[B_(f"ebz{blk}")]))
        zero_ops.append(S.op("gpd", lambda p: p.dma_start(
            out=eb_tw.rearrange("(c p) w -> p c w", p=128), in_=tid_t[:, :]),
            reads=[B_("tid")], writes=[B_("ebztw")]))
        bufs_zero_ops[:] = zero_ops
        for d in "fb":
            S.op("spd", lambda p, d=d: p.dma_start(out=acol[d][0:NH, :], in_=alog_in[d]),
                 writes=[B_("acol" + d)])
            S.op("spd", lambda p, d=d: p.dma_start(out=dtbias[d][0:NH, :], in_=dtb_in[d]),
                 writes=[B_("dtb" + d)])
            S.op("spd", lambda p, d=d: p.dma_start(out=msnap_s[d][:, :], in_=msnap_in[d]),
                 writes=[B_("msnap" + d)])
            for c in range(10):
                S.op("spd", lambda p, d=d, c=c: p.dma_start(
                    out=convw[d][:, 4 * c:4 * c + 4],
                    in_=convw_in[d][128 * c:128 * c + 128, :]),
                    writes=[B_("convw" + d)])
            S.op("spd", lambda p, d=d: p.dma_start(
                out=convb[d][:, :], in_=convb_in[d][:, :]),
                writes=[B_("convb" + d)])
            S.op("act", lambda a, d=d: a.activation(out=acol[d][0:NH, :],
                                                    in_=acol[d][0:NH, :], func=AF.Exp),
                 reads=[B_("acol" + d)], writes=[B_("acol" + d)])
            S.op("dve", lambda v, d=d: v.tensor_scalar(acol[d][0:NH, :], acol[d][0:NH, :],
                                                       -1.0, None, OP.mult),
                 reads=[B_("acol" + d)], writes=[B_("acol" + d)])
        S.op("spd", lambda p: p.dma_start(
            out=opb_c[:, :], in_=opb_in[:, :]),
            writes=[B_("opb")])
        for c in range(8):
            S.op("spd", lambda p, c=c: p.dma_start(
                out=gatew_s[:, ETOT * c:ETOT * c + ETOT],
                in_=gatew_in[128 * c:128 * c + 128, :]),
                writes=[B_("gatew")])
        S.op("spd", lambda p: p.dma_start(out=ebias_c[0:ETOT, :], in_=ebias_in),
             writes=[B_("ebias")])
        rows = [(normw_in, normw_bc, D, "normw")]
        for d in "fb":
            rows.append((gnw_in[d], gnw_bc[d], DIN, "gnw" + d))
            rows.append((dd_in[d], dcol[d], NH, "dcol" + d))
        for src, dst, n, nm in rows:
            rbuf = main.take(n)[0:1, :]
            S.op("spd", lambda p, src=src, rbuf=rbuf: p.dma_start(out=rbuf, in_=src),
                 writes=[B_("rowld_" + nm)])
            for i in range((n + 511) // 512):
                w = min(512, n - 512 * i)
                S.op("pe", lambda pe, rbuf=rbuf, i=i, w=w: pe.matmul(
                    psum[7][:, 0:w], ones[0:1, :], rbuf[:, 512 * i:512 * i + w],
                    start=True, stop=True),
                    reads=[B_("rowld_" + nm), B_("ones")], writes=pw(7, 0, w))
                S.op("dve", lambda v, dst=dst, i=i, w=w: v.tensor_copy(
                    dst[:, 512 * i:512 * i + w], psum[7][:, 0:w]),
                    reads=pw(7, 0, w), writes=[B_(nm)])
        main.reset(0)

    def phase1_pass(b, d):
        bd = 2 * b + (0 if d == "f" else 1)
        rev = (d == "b")
        hoff = 0 if d == "f" else DH
        m0 = main.mark()

        inw_s = [main.take(DIP) for _ in range(4)]
        x_tm = [[main.take(D) for _ in range(2)] for _ in range(2)]
        xa_fm = [[main.take(TT) for _ in range(4)] for _ in range(2)]
        xnf_b = [main.take(128) for _ in range(2)]
        ring = [[main.take(TT + 3) for _ in range(10)] for _ in range(2)]
        xbcs = [[main.take(TT) for _ in range(10)] for _ in range(2)]
        dtrow = [main.take(TT) for _ in range(2)]
        dtarow = [main.take(TT) for _ in range(2)]
        sp_t = [main.take(TT) for _ in range(2)]
        zbuf = [main.take(TT) for _ in range(2)]
        xh_tm = [main.take(DIN) for _ in range(2)]
        dtx_tm = [main.take(DIN) for _ in range(2)]
        dtxw = [main.take(DIN) for _ in range(2)]
        btm = [main.take(Q) for _ in range(2)]
        ebuf = [main.take(NH * Q) for _ in range(2)]
        gm = [main.take(Q) for _ in range(2)]
        _yb = main.take(DIN)
        ybuf = [_yb, _yb]
        dtm_c = [main.take(NH) for _ in range(2)]
        datm = [main.take(NH) for _ in range(2)]
        dtcol = [main.take(NH) for _ in range(2)]
        drows = [main.take(Q) for _ in range(2)]
        gcol = [main.take(NH) for _ in range(2)]
        grow = [main.take(NH) for _ in range(2)]
        s_run = main.take(DIN)
        trows = main.take(NH * Q)
        brun = main.take(1)
        ebx = main.take(NCH)
        nscr = main.take(D)
        ncols = [main.take(8) for _ in range(2)]
        hfm = main.take(16)

        bS = B_("srun")
        bBounds = B_(f"bounds{bd}")
        S.op("dve", lambda v: v.memset(s_run[:, :], 0.0), writes=[bS])

        for k in range(4):
            S.op("acd", lambda p, k=k: p.dma_start(
                out=inw_s[k][:, :], in_=inw_in[d][128 * k:128 * k + 128, :]),
                writes=[B_(f"inw{k}")])
        binw = [B_(f"inw{k}") for k in range(4)]

        def norms(xt, npart, NP, bX):
            xv = xt[0:npart, :]
            nn = ncols[NP][0:npart, :]
            sc = nscr[0:npart, :]
            bN = B_(f"ncols{NP}")
            bScr = B_("nscr")
            S.op("dve", lambda v: v.reduce_sum(nn[:, 0:1], xv, AX.X),
                 reads=[bX], writes=[bN])
            S.op("act", lambda a: a.activation(out=sc, in_=xv, func=AF.Square,
                                               accum_out=nn[:, 1:2]),
                 reads=[bX], writes=[bScr, bN])
            S.op("dve", lambda v: v.tensor_scalar(nn[:, 2:3], nn[:, 0:1],
                                                  1.0 / D, None, OP.mult),
                 reads=[bN], writes=[bN])
            S.op("dve", lambda v: v.tensor_tensor(nn[:, 4:5], nn[:, 2:3], nn[:, 2:3],
                                                  OP.mult),
                 reads=[bN], writes=[bN])
            S.op("dve", lambda v: v.tensor_scalar(nn[:, 3:4], nn[:, 1:2],
                                                  1.0 / D, nn[:, 4:5],
                                                  OP.mult, OP.subtract),
                 reads=[bN], writes=[bN])
            S.op("act", lambda a: a.activation(out=nn[:, 5:6], in_=nn[:, 3:4],
                                               func=AF.Sqrt, bias=epscol[0:npart, :]),
                 reads=[bN, B_("epscol")], writes=[bN])
            S.op("dve", lambda v: v.reciprocal(nn[:, 5:6], nn[:, 5:6]),
                 reads=[bN], writes=[bN])
            S.op("dve", lambda v: v.tensor_scalar(xv, xv, nn[:, 2:3], nn[:, 5:6],
                                                  OP.subtract, OP.mult),
                 reads=[bN], writes=[bX])
            S.op("act", lambda a: a.activation(out=sc, in_=xv, func=AF.Square,
                                               accum_out=nn[:, 6:7]),
                 reads=[bX], writes=[bScr, bN])
            S.op("act", lambda a: a.activation(out=nn[:, 7:8], in_=nn[:, 6:7],
                                               func=AF.Sqrt, scale=1.0 / D,
                                               bias=epscol[0:npart, :]),
                 reads=[bN, B_("epscol")], writes=[bN])
            S.op("dve", lambda v: v.reciprocal(nn[:, 7:8], nn[:, 7:8]),
                 reads=[bN], writes=[bN])

        hrow0 = 0 if not rev else tloc + 3
        bH = B_("xtm0_0")
        S.op("spd", lambda p: p.dma_start(out=x_tm[0][0][0:3, :],
                                          in_=x_in[b, hrow0:hrow0 + 3, :]),
             writes=[bH])
        norms(x_tm[0][0], 3, 0, bH)
        S.op("dve", lambda v: v.scalar_tensor_tensor(
            x_tm[0][0][0:3, hoff:hoff + DH], x_tm[0][0][0:3, hoff:hoff + DH],
            ncols[0][0:3, 7:8], normw_bc[0:3, hoff:hoff + DH], OP.mult, OP.mult),
            reads=[B_("ncols0"), B_("normw")], writes=[bH])
        for k in range(4):
            S.op("pe", lambda pe, k=k: pe.transpose(
                psum[4][:, 32 * k:32 * k + 3],
                x_tm[0][0][0:3, hoff + 128 * k:hoff + 128 * k + 128], ident[0:3, 0:3]),
                reads=[bH, B_("ident")], writes=pw(4, 32 * k, 3))
        for k in range(4):
            S.op("dve", lambda v, k=k: v.tensor_copy(hfm[:, 4 * k:4 * k + 3],
                                                     psum[4][:, 32 * k:32 * k + 3]),
                 reads=pw(4, 32 * k, 3), writes=[B_("hfm")])
        for c in range(10):
            bank = c % 2
            srcAP = hfm if not rev else None
            for k in range(4):
                S.op("pe", lambda pe, c=c, k=k, bank=bank: pe.matmul(
                    psum[bank][:, 0:3],
                    inw_s[k][:, 1024 + 128 * c:1024 + 128 * c + 128],
                    hfm[:, 4 * k:4 * k + 3], start=(k == 0), stop=(k == 3)),
                    reads=[B_("hfm"), binw[k]], writes=pw(bank, 0, 3))
            dst = ring[0][c][:, 0:3]
            if rev:
                dst = dst[:, ::-1]
            S.op("dve", lambda v, c=c, dst=dst, bank=bank: v.tensor_copy(
                dst, psum[bank][:, 0:3]),
                reads=pw(bank, 0, 3), writes=[B_(f"ring0_{c}")])

        if _dbgck and _dbgck.endswith(",h") and int(_dbgck.split(",")[0]) == bd:
            for di, (bn, ap) in enumerate([("hfm", hfm[:, :]), ("xtm0", x_tm[0][0][0:3, :]),
                                           ("ring8", ring[8][:, 0:8]), ("ring0", ring[0][:, 0:8]),
                                           ("ncols", ncols[0:3, :])]):
                S.op("gpd", lambda p, di=di, ap=ap: p.dma_start(
                    out=dbg_t[di, 0:ap.shape[0], 0:ap.shape[-1]], in_=ap),
                    reads=[B_(bn)], writes=[B_(f"dbgh{di}")])

        def chunk(ti, cc):
            ch = CPT * ti + cc
            t0 = Q * cc
            P = ti % 2
            cp = cc % 2
            bXB = [B_(f"xbcs{P}_{c}") for c in range(10)]
            S.op("pe", lambda pe: pe.transpose(psum[4][:, 0:128],
                                               xbcs[P][8][:, t0:t0 + Q], ident[:, :]),
                 reads=[bXB[8], B_("ident")], writes=pw(4, 0, 128))
            S.op("dve", lambda v: v.tensor_copy(btm[cp][:, :], psum[4][:, 0:128]),
                 reads=pw(4, 0, 128), writes=[B_(f"btm{cp}")])
            S.op("pe", lambda pe: pe.transpose(psum[4][:, 128:128 + NH],
                                               dtrow[P][0:NH, t0:t0 + Q], ident[0:NH, 0:NH]),
                 reads=[B_(f"dtrow{P}"), B_("ident")], writes=pw(4, 128, NH))
            S.op("dve", lambda v: v.tensor_copy(dtcol[cp][:, :], psum[4][:, 128:128 + NH]),
                 reads=pw(4, 128, NH), writes=[B_(f"dtcol{cp}")])
            S.op("pe", lambda pe: pe.transpose(psum[4][:, 256:256 + NH],
                                               dtarow[P][0:NH, t0:t0 + Q], ident[0:NH, 0:NH]),
                 reads=[B_(f"dtarow{P}"), B_("ident")], writes=pw(4, 256, NH))
            S.op("dve", lambda v: v.tensor_copy(datm[cp][:, :], psum[4][:, 256:256 + NH]),
                 reads=pw(4, 256, NH), writes=[B_(f"datm{cp}")])
            S.op("pe", lambda pe: pe.matmul(psum[6][:, 0:NH], lt[:, :], datm[cp][:, :],
                                            start=True, stop=True),
                 reads=[B_(f"datm{cp}"), B_("lt")], writes=pw(6, 0, NH))
            S.op("dve", lambda v: v.tensor_copy(dtm_c[cp][:, :], psum[6][:, 0:NH]),
                 reads=pw(6, 0, NH), writes=[B_(f"dtm{cp}")])
            S.op("act", lambda a: a.activation(out=edel[bd][ch][:, :], in_=psum[6][:, 0:NH],
                                               func=AF.Exp),
                 reads=pw(6, 0, NH), writes=[B_(f"edel{bd}_{ch}")])
            S.op("pe", lambda pe: pe.transpose(psum[6][0:NH, 128:256], dtm_c[cp][:, :],
                                               ident[:, :]),
                 reads=[B_(f"dtm{cp}"), B_("ident")], writes=pw(6, 128, 128))
            S.op("dve", lambda v: v.tensor_copy(drows[cp][0:NH, :], psum[6][0:NH, 128:256]),
                 reads=pw(6, 128, 128), writes=[B_(f"drows{cp}")])
            if ch == 0:
                S.op("dve", lambda v: v.tensor_copy(brun[0:NH, :], drows[cp][0:NH, 127:128]),
                     reads=[B_(f"drows{cp}")], writes=[B_("brun")])
            else:
                S.op("dve", lambda v: v.tensor_tensor(brun[0:NH, :], brun[0:NH, :],
                                                      drows[cp][0:NH, 127:128], OP.add),
                     reads=[B_(f"drows{cp}"), B_("brun")], writes=[B_("brun")])
            S.op("dve", lambda v, ch=ch: v.tensor_copy(bounds[bd][0:NH, ch:ch + 1],
                                                       brun[0:NH, :]),
                 reads=[B_("brun")], writes=[bBounds])
            S.op("pe", lambda pe: pe.transpose(psum[7][0:1, 32:32 + NH],
                                               drows[cp][0:NH, 127:128],
                                               ident[0:NH, 0:NH]),
                 reads=[B_(f"drows{cp}"), B_("ident")], writes=pw(7, 32, NH))
            S.op("act", lambda a: a.activation(out=grow[cp][0:1, 0:NH],
                                               in_=psum[7][0:1, 32:32 + NH],
                                               func=AF.Exp),
                 reads=pw(7, 32, NH), writes=[B_(f"grow{cp}")])
            S.op("pe", lambda pe: pe.matmul(psum[7][:, 0:NH], ones[0:1, :],
                                            grow[cp][0:1, 0:NH], start=True, stop=True),
                 reads=[B_(f"grow{cp}"), B_("ones")], writes=pw(7, 0, NH))
            S.op("dve", lambda v: v.tensor_copy(gcol[cp][:, :], psum[7][:, 0:NH]),
                 reads=pw(7, 0, NH), writes=[B_(f"gcol{cp}")])
            for fc in range(8):
                bank = 4 + (fc % 2)
                slot = 256
                S.op("pe", lambda pe, fc=fc, bank=bank, slot=slot: pe.transpose(
                    psum[bank][:, slot:slot + 128], xbcs[P][fc][:, t0:t0 + Q], ident[:, :]),
                    reads=[bXB[fc], B_("ident")], writes=pw(bank, slot, 128))
                S.op("dve", lambda v, fc=fc, bank=bank, slot=slot: v.tensor_copy(
                    xh_tm[cp][:, 128 * fc:128 * fc + 128], psum[bank][:, slot:slot + 128]),
                    reads=pw(bank, slot, 128), writes=[B_(f"xh{cp}")])
            S.op("dve", lambda v: v.tensor_tensor(
                dtx_tm[cp].rearrange("p (h q) -> p h q", h=NH),
                xh_tm[cp].rearrange("p (h q) -> p h q", h=NH),
                dtcol[cp].unsqueeze(2).broadcast_to((128, NH, HDIM)), OP.mult),
                reads=[B_(f"xh{cp}"), B_(f"dtcol{cp}")], writes=[B_(f"dtx{cp}")])
            S.op("spd", lambda p: p.dma_start(
                out=trows[0:1, 0:NH * Q], in_=drows[cp][0:NH, 0:Q]),
                reads=[B_(f"drows{cp}")], writes=[B_("trows")])
            for half in range(2):
                for j2 in range(2):
                    bank = 2 + j2
                    S.op("pe", lambda pe, half=half, j2=j2, bank=bank: pe.matmul(
                        psum[bank][:, 0:512], ones[0:1, :],
                        trows[0:1, 1024 * half + 512 * j2:1024 * half + 512 * j2 + 512],
                        start=True, stop=True),
                        reads=[B_("trows"), B_("ones")], writes=pw(bank, 0, 512))
                for j2 in range(2):
                    bank = 2 + j2
                    h0 = 8 * half + 4 * j2
                    S.op("dve", lambda v, half=half, j2=j2, bank=bank, h0=h0: v.tensor_tensor(
                        ebuf[cp][:, 512 * (2 * half + j2):512 * (2 * half + j2) + 512]
                            .rearrange("p (h t) -> p h t", h=4),
                        psum[bank][:, 0:512].rearrange("p (h t) -> p h t", h=4),
                        dtm_c[cp][:, h0:h0 + 4].unsqueeze(2).broadcast_to((128, 4, 128)),
                        OP.subtract),
                        reads=pw(bank, 0, 512) + [B_(f"dtm{cp}")], writes=[B_(f"ebuf{cp}")])
            S.op("dve", lambda v: v.tensor_scalar(
                ebuf[cp][:, :], ebuf[cp][:, :], 0.0, None, OP.min),
                reads=[B_(f"ebuf{cp}")], writes=[B_(f"ebuf{cp}")])
            S.op("act", lambda a: a.activation(out=ebuf[cp][:, :], in_=ebuf[cp][:, :], func=AF.Exp),
                 reads=[B_(f"ebuf{cp}")], writes=[B_(f"ebuf{cp}")])
            S.op("dve", lambda v: v.tensor_tensor(
                dtxw[cp].rearrange("p (h q) -> p h q", h=NH),
                dtx_tm[cp].rearrange("p (h q) -> p h q", h=NH),
                ebuf[cp][:, 127::128].unsqueeze(2).broadcast_to((128, NH, HDIM)),
                OP.mult),
                reads=[B_(f"dtx{cp}"), B_(f"ebuf{cp}")], writes=[B_(f"dtxw{cp}")])
            S.op("pe", lambda pe: pe.matmul(psum[6][:, 256:384], xbcs[P][8][:, t0:t0 + Q],
                                            xbcs[P][9][:, t0:t0 + Q], start=True, stop=True),
                 reads=[bXB[8], bXB[9]], writes=pw(6, 256, 128))
            S.op("dve", lambda v: v.tensor_tensor(gm[cp][:, :], psum[6][:, 256:384],
                                                  lt[:, :], OP.mult),
                 reads=pw(6, 256, 128) + [B_("lt")], writes=[B_(f"gm{cp}")])
            S.op("dve", lambda v: v.tensor_tensor(
                ebuf[cp].rearrange("p (h t) -> p h t", h=NH),
                ebuf[cp].rearrange("p (h t) -> p h t", h=NH),
                gm[cp].unsqueeze(1).broadcast_to((128, NH, Q)), OP.mult),
                reads=[B_(f"ebuf{cp}"), B_(f"gm{cp}")], writes=[B_(f"ebuf{cp}")])
            S.op("gpd", lambda p, ch=ch: p.dma_start(out=sc_hl[bd, ch, :, :],
                                                     in_=s_run[:, :]),
                 reads=[bS], writes=[B_(f"schl{bd}_{ch}")])
            for half in range(2):
                S.op("pe", lambda pe, half=half: pe.matmul(
                    psum[half][:, 0:512], btm[cp][:, :], dtxw[cp][:, 512 * half:512 * half + 512],
                    start=True, stop=True),
                    reads=[B_(f"btm{cp}"), B_(f"dtxw{cp}")], writes=pw(half, 0, 512))
            S.op("dve", lambda v: v.tensor_tensor(
                s_run.rearrange("p (h q) -> p h q", h=NH),
                s_run.rearrange("p (h q) -> p h q", h=NH),
                gcol[cp].unsqueeze(2).broadcast_to((128, NH, HDIM)), OP.mult),
                reads=[B_(f"gcol{cp}")], writes=[bS])
            for half in range(2):
                S.op("dve", lambda v, half=half: v.tensor_tensor(
                    s_run[:, 512 * half:512 * half + 512],
                    s_run[:, 512 * half:512 * half + 512],
                    psum[half][:, 0:512], OP.add),
                    reads=pw(half, 0, 512), writes=[bS])
            for h in range(NH):
                bank = 2 + h // 8
                off = 64 * (h % 8)
                S.op("pe", lambda pe, h=h, bank=bank, off=off: pe.matmul(
                    psum[bank][:, off:off + 64], ebuf[cp][:, 128 * h:128 * h + 128],
                    dtx_tm[cp][:, 64 * h:64 * h + 64],
                    start=(h % 8 == 0), stop=(h % 8 == 7)),
                    reads=[B_(f"ebuf{cp}"), B_(f"dtx{cp}")],
                    writes=pw(bank, 0, 512) if h % 8 == 0 else pw(bank, off, 64))
            S.op("dve", lambda v: v.tensor_tensor(
                ybuf[cp].rearrange("p (h q) -> p h q", h=NH),
                xh_tm[cp].rearrange("p (h q) -> p h q", h=NH),
                dcol[d].unsqueeze(2).broadcast_to((128, NH, HDIM)), OP.mult),
                reads=[B_(f"xh{cp}"), B_("dcol" + d)], writes=[B_("ybuf")])
            for half in range(2):
                bank = 2 + half
                S.op("dve", lambda v, half=half, bank=bank: v.tensor_tensor(
                    ybuf[cp][:, 512 * half:512 * half + 512],
                    ybuf[cp][:, 512 * half:512 * half + 512],
                    psum[bank][:, 0:512], OP.add),
                    reads=pw(bank, 0, 512), writes=[B_("ybuf")])
            S.op("gpd", lambda p, ch=ch: p.dma_start(out=sc_y[bd, ch, :, :],
                                                     in_=ybuf[cp][:, :]),
                 reads=[B_("ybuf")], writes=[B_(f"scy{bd}_{ch}")])
        def tile_body(ti):
            P = ti % 2
            for k in range(CPT):
                sblk = CPT * ti + k
                oblk = sblk if not rev else (tloc // 128 - 1 - sblk)
                r0 = 3 + 128 * oblk
                bX = B_(f"xtm{P}_{k}")
                S.op("spd", lambda p, k=k, r0=r0: p.dma_start(
                    out=x_tm[P][k][:, :], in_=x_in[b, r0:r0 + 128, :]), writes=[bX])
                norms(x_tm[P][k], 128, P, bX)
                if d == "f":
                    gtok = b * tloc + 128 * oblk
                    for fc in range(8):
                        bank = 4 + (fc % 2)
                        S.op("pe", lambda pe, fc=fc, k=k, bank=bank: pe.transpose(
                            psum[bank][:, 0:128],
                            x_tm[P][k][:, 128 * fc:128 * fc + 128], ident[:, :]),
                            reads=[bX, B_("ident")], writes=pw(bank, 0, 128))
                        S.op("dve", lambda v, fc=fc, bank=bank: v.tensor_copy(
                            xnf_b[P][:, :], psum[bank][:, 0:128]),
                            reads=pw(bank, 0, 128), writes=[B_(f"xnf{P}")])
                        S.op("gpd", lambda p, fc=fc, gtok=gtok: p.dma_start(
                            out=sc_xn[fc, :, gtok:gtok + 128], in_=xnf_b[P][:, :]),
                            reads=[B_(f"xnf{P}")], writes=[B_(f"scxn{fc}")])
                S.op("dve", lambda v, k=k: v.scalar_tensor_tensor(
                    x_tm[P][k][:, hoff:hoff + DH], x_tm[P][k][:, hoff:hoff + DH],
                    ncols[P][:, 7:8], normw_bc[:, hoff:hoff + DH], OP.mult, OP.mult),
                    reads=[B_(f"ncols{P}"), B_("normw")], writes=[bX])
                for fc in range(4):
                    bank = 4 + (fc % 2)
                    S.op("pe", lambda pe, fc=fc, k=k, bank=bank: pe.transpose(
                        psum[bank][:, 128:256],
                        x_tm[P][k][:, hoff + 128 * fc:hoff + 128 * fc + 128], ident[:, :]),
                        reads=[bX, B_("ident")], writes=pw(bank, 128, 128))
                    dst = xa_fm[P][fc][:, 128 * k:128 * k + 128]
                    if rev:
                        dst = dst[:, ::-1]
                    S.op("dve", lambda v, dst=dst, bank=bank: v.tensor_copy(
                        dst, psum[bank][:, 128:256]),
                        reads=pw(bank, 128, 128), writes=[B_(f"xafm{P}_{fc}")])

            rb = [B_(f"xafm{P}_{fc}") for fc in range(4)] + binw
            for oc in range(19):
                pn = oc % 2
                nr = 128 if oc < 18 else NH
                for kk in range(4):
                    S.op("pe", lambda pe, oc=oc, kk=kk, pn=pn, nr=nr: pe.matmul(
                        psum[pn][0:nr, 0:TT],
                        inw_s[kk][:, 128 * oc:128 * oc + nr],
                        xa_fm[P][kk][:, :], start=(kk == 0), stop=(kk == 3)),
                        reads=rb, writes=pw(pn, 0, TT))
                if oc < 8:
                    S.op("act", lambda a, pn=pn: a.activation(
                        out=zbuf[P][:, :], in_=psum[pn][:, 0:TT], func=AF.Silu),
                        reads=pw(pn, 0, TT), writes=[B_(f"zbuf{P}")])
                    S.op("gpd", lambda p, oc=oc, ti=ti: p.dma_start(
                        out=sc_z[bd, oc, :, TT * ti:TT * ti + TT], in_=zbuf[P][:, :]),
                        reads=[B_(f"zbuf{P}")], writes=[B_(f"scz{bd}_{oc}")])
                elif oc < 18:
                    c = oc - 8
                    S.op("dve", lambda v, c=c, pn=pn: v.tensor_copy(
                        ring[P][c][:, 3:3 + TT], psum[pn][:, 0:TT]),
                        reads=pw(pn, 0, TT), writes=[B_(f"ring{c}")])
                else:
                    pbr = pw(pn, 0, TT)
                    S.op("act", lambda a, pn=pn: a.activation(
                        out=sp_t[0][0:NH, :], in_=psum[pn][0:NH, 0:TT],
                        func=AF.Abs, bias=dtbias[d][0:NH, :]),
                        reads=pbr + [B_("dtb" + d)], writes=[B_("sp0")])
                    S.op("act", lambda a: a.activation(
                        out=sp_t[0][0:NH, :], in_=sp_t[0][0:NH, :], func=AF.Exp,
                        scale=-1.0),
                        reads=[B_("sp0")], writes=[B_("sp0")])
                    S.op("act", lambda a: a.activation(
                        out=sp_t[0][0:NH, :], in_=sp_t[0][0:NH, :], func=AF.Ln,
                        bias=onecol[0:NH, :]),
                        reads=[B_("sp0"), B_("onecol")], writes=[B_("sp0")])
                    S.op("act", lambda a, pn=pn: a.activation(
                        out=sp_t[1][0:NH, :], in_=psum[pn][0:NH, 0:TT],
                        func=AF.Relu, bias=dtbias[d][0:NH, :]),
                        reads=pbr + [B_("dtb" + d)], writes=[B_("sp1")])
                    S.op("dve", lambda v: v.tensor_tensor(
                        dtrow[P][0:NH, :], sp_t[0][0:NH, :], sp_t[1][0:NH, :], OP.add),
                        reads=[B_("sp0"), B_("sp1")], writes=[B_(f"dtrow{P}")])
            for c in range(10):
                rbc = B_(f"ring{P}_{c}")
                xb = B_(f"xbcs{P}_{c}")
                S.op("dve", lambda v, c=c: v.tensor_scalar(
                    xbcs[P][c][:, :], ring[P][c][:, 0:TT], convw[d][:, 4 * c:4 * c + 1],
                    convb[d][:, c:c + 1], OP.mult, OP.add),
                    reads=[rbc, B_("convw" + d), B_("convb" + d)], writes=[xb])
                for kk in range(1, 4):
                    S.op("dve", lambda v, c=c, kk=kk: v.scalar_tensor_tensor(
                        xbcs[P][c][:, :], ring[P][c][:, kk:kk + TT],
                        convw[d][:, 4 * c + kk:4 * c + kk + 1], xbcs[P][c][:, :],
                        OP.mult, OP.add),
                        reads=[rbc, B_("convw" + d), xb], writes=[xb])
                S.op("act", lambda a, c=c: a.activation(
                    out=xbcs[P][c][:, :], in_=xbcs[P][c][:, :], func=AF.Silu),
                    reads=[xb], writes=[xb])
                S.op("dve", lambda v, c=c, P=P: v.tensor_copy(ring[1 - P][c][:, 0:3],
                                                         ring[P][c][:, TT:TT + 3]),
                     reads=[rbc], writes=[B_(f"ring{1 - P}_{c}")])
            S.op("gpd", lambda p, ti=ti: p.dma_start(
                out=sc_c[bd, :, TT * ti:TT * ti + TT], in_=xbcs[P][9][:, :]),
                reads=[B_(f"xbcs{P}_9")], writes=[B_(f"scc{bd}")])
            S.op("dve", lambda v: v.tensor_scalar(
                dtarow[P][0:NH, :], dtrow[P][0:NH, :], acol[d][0:NH, :], None, OP.mult),
                reads=[B_(f"dtrow{P}"), B_("acol" + d)], writes=[B_(f"dtarow{P}")])
            for cc in range(CPT):
                chunk(ti, cc)

        for ti in range(NTILE):
            tile_body(ti)

        S.op("act", lambda a: a.activation(out=ebt[bd][0:NH, 0:NCH],
                                           in_=bounds[bd][0:NH, 0:NCH], func=AF.Exp),
             reads=[bBounds], writes=[B_(f"ebt{bd}")])
        S.op("pe", lambda pe: pe.transpose(psum[6][0:1, 64:64 + NH],
                                           ebt[bd][0:NH, NCH - 1:NCH],
                                           ident[0:NH, 0:NH]),
             reads=[B_(f"ebt{bd}"), B_("ident")], writes=pw(6, 64, NH))
        S.op("dve", lambda v: v.tensor_copy(grow[0][0:1, 0:NH], psum[6][0:1, 64:64 + NH]),
             reads=pw(6, 64, NH), writes=[B_("grow0")])
        S.op("pe", lambda pe: pe.matmul(psum[6][:, 64:64 + NH], ones[0:1, :],
                                        grow[0][0:1, 0:NH], start=True, stop=True),
             reads=[B_("grow0"), B_("ones")], writes=pw(6, 64, NH))
        S.op("dve", lambda v: v.tensor_copy(gam_ag[:, 16 * bd:16 * bd + 16],
                                            psum[6][:, 64:64 + NH]),
             reads=pw(6, 64, NH), writes=[B_("gamag")])
        S.op("gpd", lambda p: p.dma_start(out=ag_in[:, DIN * bd:DIN * bd + DIN],
                                          in_=s_run[:, :]),
             reads=[bS], writes=[B_("agin")])
        main.reset(m0)

    def phase2():
        NB = TT2 // 128
        m0 = main.mark()
        xn_s = [main.take(TT2) for _ in range(8)]
        xn_tm = [main.take(D) for _ in range(NB)]
        lgf = main.take(TT2)
        ltm = main.take(ETOT)
        tcol = main.take(8)
        wvt = main.take(ETOT)
        eqt = [main.take(ETOT) for _ in range(2)]
        maskp = main.take(ETOT)
        posg = main.take(ETOT)
        runbc = main.take(ETOT)
        roff = main.take(4)
        rrow = main.take(ETOT)
        destf = main.take(8)
        dscr = main.take(ETOT)
        idxu = main.take(2)
        tokw = [main.take(2) for _ in range(2)]
        acct = main.take(D)

        S.op("dve", lambda v: v.memset(roff[0:ETOT, 0:1], 0.0), writes=[B_("roff")])
        scat_ops = []
        init_ops = []
        for t2 in range(NT2):
            g0 = TT2 * t2
            bXNs = [B_(f"p2xn{fc}") for fc in range(8)]
            for fc in range(8):
                S.op("spd", lambda p, fc=fc, g0=g0: p.dma_start(
                    out=xn_s[fc][:, :], in_=sc_xn[fc, :, g0:g0 + TT2]),
                    reads=[B_(f"scxn{fc}")], writes=[bXNs[fc]])
            for tc in range(NB):
                for fc in range(8):
                    bank = 4 + fc % 2
                    S.op("pe", lambda pe, tc=tc, fc=fc, bank=bank: pe.transpose(
                        psum[bank][:, 0:128], xn_s[fc][:, 128 * tc:128 * tc + 128],
                        ident[:, :]),
                        reads=[bXNs[fc], B_("ident")], writes=pw(bank, 0, 128))
                    S.op("dve", lambda v, tc=tc, fc=fc, bank=bank: v.tensor_copy(
                        xn_tm[tc][:, 128 * fc:128 * fc + 128], psum[bank][:, 0:128]),
                        reads=pw(bank, 0, 128), writes=[B_(f"xntm{tc}")])
            for kc in range(8):
                S.op("pe", lambda pe, kc=kc: pe.matmul(
                    psum[6][0:ETOT, 0:TT2], gatew_s[:, ETOT * kc:ETOT * kc + ETOT],
                    xn_s[kc][:, :], start=(kc == 0), stop=(kc == 7)),
                    reads=[bXNs[kc], B_("gatew")], writes=pw(6, 0, TT2))
            S.op("dve", lambda v: v.tensor_scalar(
                lgf[0:ETOT, :], psum[6][0:ETOT, 0:TT2], ebias_c[0:ETOT, :], None, OP.add),
                reads=pw(6, 0, TT2) + [B_("ebias")], writes=[B_("lgf")])
            for kb in range(NB):
                gblk = NB * t2 + kb
                S.op("pe", lambda pe, kb=kb: pe.transpose(
                    psum[7][:, 0:ETOT], lgf[0:ETOT, 128 * kb:128 * kb + 128],
                    ident[0:ETOT, 0:ETOT]),
                    reads=[B_("lgf"), B_("ident")], writes=pw(7, 0, ETOT))
                S.op("dve", lambda v: v.tensor_copy(ltm[:, 0:ETOT], psum[7][:, 0:ETOT]),
                     reads=pw(7, 0, ETOT), writes=[B_("ltm")])
                lv = ltm[:, 0:ETOT]
                bL, bT, bW = B_("ltm"), B_("tcol"), B_("wvt")
                S.op("dve", lambda v, lv=lv: v.reduce_max(tcol[:, 0:1], lv, AX.X),
                     reads=[bL], writes=[bT])
                S.op("dve", lambda v, lv=lv: v.tensor_scalar(
                    eqt[0][:, :], lv, tcol[:, 0:1], None, OP.is_equal),
                    reads=[bL, bT], writes=[B_("eq0")])
                S.op("dve", lambda v, lv=lv: v.scalar_tensor_tensor(
                    wvt[:, :], eqt[0][:, :], -1e30, lv, OP.mult, OP.add),
                    reads=[bL, B_("eq0")], writes=[bW])
                S.op("dve", lambda v: v.reduce_max(tcol[:, 1:2], wvt[:, :], AX.X),
                     reads=[bW], writes=[bT])
                S.op("dve", lambda v, lv=lv: v.tensor_scalar(
                    eqt[1][:, :], lv, tcol[:, 1:2], None, OP.is_equal),
                    reads=[bL, bT], writes=[B_("eq1")])
                S.op("dve", lambda v: v.tensor_scalar(
                    tcol[:, 2:3], tcol[:, 0:1], tcol[:, 1:2], None, OP.subtract),
                    reads=[bT], writes=[bT])
                S.op("act", lambda a: a.activation(out=tcol[:, 3:4], in_=tcol[:, 2:3],
                                                   func=AF.Sigmoid),
                     reads=[bT], writes=[bT])
                S.op("dve", lambda v: v.tensor_scalar(
                    tcol[:, 4:5], tcol[:, 3:4], -1.0, 1.0, OP.mult, OP.add),
                    reads=[bT], writes=[bT])
                S.op("dve", lambda v: v.tensor_tensor(maskp[:, :], eqt[0][:, :],
                                                      eqt[1][:, :], OP.add),
                     reads=[B_("eq0"), B_("eq1")], writes=[B_("maskp")])
                S.op("pe", lambda pe: pe.matmul(psum[6][0:ETOT, 256:257], maskp[:, :],
                                                ones[:, 0:1], start=True, stop=True),
                     reads=[B_("maskp"), B_("ones")], writes=pw(6, 256, 1))
                S.op("pe", lambda pe: pe.matmul(psum[6][:, 128:128 + ETOT], ltst[:, :],
                                                maskp[:, :], start=True, stop=True),
                     reads=[B_("maskp"), B_("ltst")], writes=pw(6, 128, ETOT))
                S.op("pe", lambda pe: pe.transpose(psum[6][0:1, 384:384 + ETOT],
                                                   roff[0:ETOT, 0:1], ident[0:ETOT, 0:ETOT]),
                     reads=[B_("roff"), B_("ident")], writes=pw(6, 384, ETOT))
                S.op("dve", lambda v: v.tensor_copy(rrow[0:1, :], psum[6][0:1, 384:384 + ETOT]),
                     reads=pw(6, 384, ETOT), writes=[B_("rrow")])
                S.op("pe", lambda pe: pe.matmul(psum[7][:, 32:32 + ETOT], ones[0:1, :],
                                                rrow[0:1, :], start=True, stop=True),
                     reads=[B_("rrow"), B_("ones")], writes=pw(7, 32, ETOT))
                S.op("dve", lambda v: v.tensor_copy(runbc[:, :], psum[7][:, 32:32 + ETOT]),
                     reads=pw(7, 32, ETOT), writes=[B_("runbc")])
                S.op("dve", lambda v: v.tensor_tensor(posg[:, :], psum[6][:, 128:128 + ETOT],
                                                      runbc[:, :], OP.add),
                     reads=pw(6, 128, ETOT) + [B_("runbc")], writes=[B_("posg")])
                S.op("dve", lambda v: v.tensor_tensor(roff[0:ETOT, 0:1], roff[0:ETOT, 0:1],
                                                      psum[6][0:ETOT, 256:257], OP.add),
                     reads=pw(6, 256, 1) + [B_("roff")], writes=[B_("roff")])
                for k in range(2):
                    bE = B_(f"eq{k}")
                    S.op("dve", lambda v, k=k: v.tensor_tensor(
                        dscr[:, :], eqt[k][:, :], iota12bc[:, :], OP.mult),
                        reads=[bE, B_("iota12")], writes=[B_("dscr")])
                    S.op("dve", lambda v, k=k: v.reduce_sum(
                        destf[:, k:k + 1], dscr[:, :], AX.X),
                        reads=[B_("dscr")], writes=[B_("destf")])
                    S.op("dve", lambda v, k=k: v.tensor_tensor(
                        dscr[:, :], eqt[k][:, :], posg[:, :], OP.mult),
                        reads=[bE, B_("posg")], writes=[B_("dscr")])
                    S.op("dve", lambda v, k=k: v.reduce_sum(
                        destf[:, 2 + k:3 + k], dscr[:, :], AX.X),
                        reads=[B_("dscr")], writes=[B_("destf")])
                    S.op("dve", lambda v, k=k: v.tensor_scalar(
                        destf[:, k:k + 1], destf[:, k:k + 1], float(CAP),
                        destf[:, 2 + k:3 + k], OP.mult, OP.add),
                        reads=[B_("destf")], writes=[B_("destf")])
                    S.op("dve", lambda v, k=k: v.tensor_copy(
                        idxu[:, k:k + 1].bitcast(mybir.dt.uint32), destf[:, k:k + 1]),
                        reads=[B_("destf")], writes=[B_("idxu")])
                    S.op("dve", lambda v, k=k, gblk=gblk: v.tensor_scalar(
                        tokw[k][:, 0:1], idxp[:, 0:1], 1.0, float(128 * gblk),
                        OP.mult, OP.add),
                        reads=[B_("idxp")], writes=[B_(f"tokw{k}")])
                    S.op("dve", lambda v, k=k: v.tensor_copy(tokw[k][:, 1:2],
                                                             tcol[:, 3 + k:4 + k]),
                         reads=[bT], writes=[B_(f"tokw{k}")])
                    S.op("gpd", lambda p, k=k, kb=kb: p.indirect_dma_start(
                        eb_x, bass.IndirectOffsetOnAxis(
                            ap=idxu[:, k:k + 1].bitcast(mybir.dt.uint32), axis=0),
                        xn_tm[kb][:, :], None),
                        reads=[B_("idxu"), B_(f"xntm{kb}"), B_("ebz")],
                        writes=[B_("ebsc")])
                    S.op("gpd", lambda p, k=k: p.indirect_dma_start(
                        eb_tw, bass.IndirectOffsetOnAxis(ap=idxu[:, k:k + 1].bitcast(mybir.dt.uint32), axis=0),
                        tokw[k][:, :], None),
                        reads=[B_("idxu"), B_(f"tokw{k}"), B_("ebz")],
                        writes=[B_("ebsc")])
                S.op("dve", lambda v: v.tensor_scalar(
                    eqt[0][:, 8:ETOT], eqt[0][:, 8:ETOT], tcol[:, 3:4], None, OP.mult),
                    reads=[B_("eq0"), bT], writes=[B_("eq0")])
                S.op("dve", lambda v: v.tensor_scalar(eqt[1][:, 8:ETOT], eqt[1][:, 8:ETOT],
                                                      tcol[:, 4:5], None, OP.mult),
                     reads=[B_("eq1"), bT], writes=[B_("eq1")])
                S.op("dve", lambda v: v.tensor_tensor(eqt[0][:, 8:ETOT], eqt[0][:, 8:ETOT],
                                                      eqt[1][:, 8:ETOT], OP.add),
                     reads=[B_("eq0"), B_("eq1")], writes=[B_("eq0")])
                S.op("dve", lambda v: v.reduce_sum(tcol[:, 5:6], eqt[0][:, 8:ETOT], AX.X),
                     reads=[B_("eq0")], writes=[bT])
                S.op("dve", lambda v, kb=kb: v.tensor_scalar(
                    acct[:, :], xn_tm[kb][:, :], tcol[:, 5:6], None, OP.mult),
                    reads=[B_(f"xntm{kb}"), bT], writes=[B_("acct")])
                init_ops.append(S.op("gpd", lambda p, gblk=gblk: p.dma_start(
                    out=acc_h[128 * gblk:128 * gblk + 128, :], in_=acct[:, :]),
                    reads=[B_("acct")], writes=[B_(f"acci{gblk}")]))
        main.reset(m0)
        S.fence()

        m1 = main.mark()
        w1_s = [main.take(DFF) for _ in range(8)]
        w2_s = [main.take(D) for _ in range(16)]
        h_s = [main.take(256) for _ in range(16)]
        xg_tm = [main.take(D) for _ in range(2)]
        xg_fm = [main.take(256) for _ in range(8)]
        twg = [main.take(2) for _ in range(2)]
        otile = main.take(D)
        oidx = main.take(1)

        NQ = CAP // 256
        for e in range(NEXP):
            bW1 = [B_(f"w1s{kc}") for kc in range(8)]
            bW2 = [B_(f"w2s{dc}") for dc in range(16)]
            for kc in range(8):
                S.op("acd", lambda p, e=e, kc=kc: p.dma_start(
                    out=w1_s[kc][:, :], in_=w1t_in[e, 128 * kc:128 * kc + 128, :]),
                    writes=[bW1[kc]])
            for dc in range(16):
                S.op("acd", lambda p, e=e, dc=dc: p.dma_start(
                    out=w2_s[dc][:, :], in_=w2t_in[e, 128 * dc:128 * dc + 128, :]),
                    writes=[bW2[dc]])
            for q in range(NQ):
                r0 = e * CAP + 256 * q
                for b2 in range(2):
                    S.op("spd", lambda p, r0=r0, b2=b2: p.dma_start(
                        out=xg_tm[b2][:, :], in_=eb_x[r0 + 128 * b2:r0 + 128 * b2 + 128, :]),
                        extra=scat_ops, writes=[B_(f"xgtm{b2}")])
                    S.op("spd", lambda p, r0=r0, b2=b2: p.dma_start(
                        out=twg[b2][:, :], in_=eb_tw[r0 + 128 * b2:r0 + 128 * b2 + 128, :]),
                        extra=scat_ops, writes=[B_(f"twg{b2}")])
                    for fc in range(8):
                        bank = 4 + fc % 2
                        S.op("pe", lambda pe, b2=b2, fc=fc, bank=bank: pe.transpose(
                            psum[bank][:, 0:128], xg_tm[b2][:, 128 * fc:128 * fc + 128],
                            ident[:, :]),
                            reads=[B_(f"xgtm{b2}"), B_("ident")], writes=pw(bank, 0, 128))
                        S.op("dve", lambda v, b2=b2, fc=fc, bank=bank: v.tensor_copy(
                            xg_fm[fc][:, 128 * b2:128 * b2 + 128], psum[bank][:, 0:128]),
                            reads=pw(bank, 0, 128), writes=[B_(f"xgfm{fc}")])
                for dc in range(16):
                    pn = dc % 2
                    for kc in range(8):
                        S.op("pe", lambda pe, dc=dc, kc=kc, pn=pn: pe.matmul(
                            psum[pn][:, 0:256], w1_s[kc][:, 128 * dc:128 * dc + 128],
                            xg_fm[kc][:, :], start=(kc == 0), stop=(kc == 7)),
                            reads=[B_(f"xgfm{kc}"), bW1[kc]], writes=pw(pn, 0, 256))
                    S.op("act", lambda a, dc=dc, pn=pn: a.activation(
                        out=h_s[dc][:, :], in_=psum[pn][:, 0:256], func=AF.Silu),
                        reads=pw(pn, 0, 256), writes=[B_(f"h{dc}")])
                for b2 in range(2):
                    for dh in range(2):
                        pn = 2 + dh
                        for dc in range(16):
                            S.op("pe", lambda pe, b2=b2, dh=dh, dc=dc, pn=pn: pe.matmul(
                                psum[pn][:, 0:512],
                                h_s[dc][:, 128 * b2:128 * b2 + 128],
                                w2_s[dc][:, 512 * dh:512 * dh + 512],
                                start=(dc == 0), stop=(dc == 15)),
                                reads=[B_(f"h{dc}"), bW2[dc]], writes=pw(pn, 0, 512))
                        S.op("dve", lambda v, b2=b2, dh=dh, pn=pn: v.tensor_scalar(
                            otile[:, 512 * dh:512 * dh + 512], psum[pn][:, 0:512],
                            twg[b2][:, 1:2], None, OP.mult),
                            reads=pw(pn, 0, 512) + [B_(f"twg{b2}")], writes=[B_("otile")])
                    S.op("dve", lambda v, b2=b2: v.tensor_copy(
                        oidx[:, 0:1].bitcast(mybir.dt.uint32), twg[b2][:, 0:1]),
                        reads=[B_(f"twg{b2}")], writes=[B_("oidx")])
                    S.op("gpd", lambda p: p.indirect_dma_start(
                        acc_h, bass.IndirectOffsetOnAxis(ap=oidx[:, 0:1].bitcast(mybir.dt.uint32), axis=0),
                        otile[:, :], None, compute_op=OP.add),
                        reads=[B_("oidx"), B_("otile")], extra=init_ops,
                        writes=[B_("accsc")])
        main.reset(m1)
        S.fence()

        m2 = main.mark()
        gtile = main.take(D)
        for gblk in range(B * tloc // 128):
            S.op("spd", lambda p, gblk=gblk: p.dma_start(
                out=gtile[:, :], in_=acc_h[128 * gblk:128 * gblk + 128, :]),
                reads=[B_("accsc")], writes=[B_("gtile")])
            S.op("act", lambda a: a.activation(out=gtile[:, :], in_=gtile[:, :],
                                               func=AF.Silu),
                 reads=[B_("gtile")], writes=[B_("gtile")])
            S.op("gpd", lambda p, gblk=gblk: p.dma_start(
                out=sc_g[128 * gblk:128 * gblk + 128, :], in_=gtile[:, :]),
                reads=[B_("gtile")], writes=[B_(f"scgb{gblk}")])
        main.reset(m2)

    def phase3():
        m0 = main.mark()
        outpw_s = {d: [main.take(DH) for _ in range(8)] for d in "fb"}
        opw_s = [main.take(D) for _ in range(8)]
        h0 = [main.take(DIN) for _ in range(4)]
        hrun = main.take(DIN)
        sblk = main.take(DIN)
        gblk = main.take(NH)
        scr3 = main.take(DIN)
        c3 = main.take(8)
        hle = {dd: main.take(DIN) for dd in "fb"}
        cch = {dd: main.take(Q) for dd in "fb"}
        yld = {dd: main.take(DIN) for dd in "fb"}
        ycb = {dd: main.take(DIN) for dd in "fb"}
        ynm = {dd: main.take(DIN) for dd in "fb"}
        yfm = {dd: main.take(DIN) for dd in "fb"}
        zld = main.take(DIN)
        ecol = main.take(NH)
        erow3 = main.take(NH)
        cat = main.take(D)
        sfm = main.take(D)
        gld = main.take(D)
        xnld = main.take(D)
        xld = main.take(D)
        otm = main.take(D)
        pfm = main.take(D)

        for d in "fb":
            for kc in range(8):
                S.op("acd", lambda p, d=d, kc=kc: p.dma_start(
                    out=outpw_s[d][kc][:, :], in_=outpw_in[d][128 * kc:128 * kc + 128, :]),
                    writes=[B_(f"outpw{d}{kc}")])
        for kc in range(8):
            S.op("acd", lambda p, kc=kc: p.dma_start(
                out=opw_s[kc][:, :], in_=opw_in[128 * kc:128 * kc + 128, :]),
                writes=[B_(f"opw{kc}")])

        for bd in range(4):
            d = "f" if bd % 2 == 0 else "b"
            bH0 = B_(f"h0_{bd}")
            bHr = B_("hrun")
            S.op("dve", lambda v, bd=bd: v.memset(h0[bd][:, :], 0.0), writes=[bH0])
            S.op("dve", lambda v: v.memset(hrun[:, :], 0.0), writes=[bHr])
            order = list(range(ncores_run)) if d == "f" else list(range(ncores_run - 1, -1, -1))
            for j in order:
                S.op("spd", lambda p, j=j, bd=bd: p.dma_start(
                    out=sblk[:, :], in_=ag_out[128 * j:128 * j + 128, DIN * bd:DIN * bd + DIN]),
                    reads=[B_("agout")], writes=[B_("sblk")])
                S.op("spd", lambda p, j=j, bd=bd: p.dma_start(
                    out=gblk[:, :],
                    in_=ag_out[128 * j:128 * j + 128, 4 * DIN + 16 * bd:4 * DIN + 16 * bd + 16]),
                    reads=[B_("agout")], writes=[B_("gblk")])
                S.op("dve", lambda v: v.tensor_tensor(
                    hrun.rearrange("p (h q) -> p h q", h=NH),
                    hrun.rearrange("p (h q) -> p h q", h=NH),
                    gblk.unsqueeze(2).broadcast_to((128, NH, HDIM)), OP.mult),
                    reads=[B_("gblk")], writes=[bHr])
                S.op("dve", lambda v: v.tensor_tensor(hrun[:, :], hrun[:, :],
                                                      sblk[:, :], OP.add),
                     reads=[B_("sblk")], writes=[bHr])
                S.op("dve", lambda v, j=j, bd=bd, d=d: v.scalar_tensor_tensor(
                    h0[bd][:, :], hrun[:, :], msnap_s[d][:, j:j + 1], h0[bd][:, :],
                    OP.mult, OP.add),
                    reads=[bHr, B_("msnap" + d)], writes=[bH0])

        def member(b, d, ch):
            bd = 2 * b + (0 if d == "f" else 1)
            mbk = 0 if d == "f" else 1
            bHle = B_("hle" + d)
            S.op("spd", lambda p: p.dma_start(out=hle[d][:, :], in_=sc_hl[bd, ch, :, :]),
                 reads=[B_(f"schl{bd}_{ch}")], writes=[bHle])
            if ch == 0:
                ec = ones16
                bEc = B_("ones16")
            else:
                ec = ecol
                bEc = B_("ecol")
                S.op("pe", lambda pe, bd=bd, ch=ch: pe.transpose(
                    psum[6 + mbk][0:1, 32:32 + NH], ebt[bd][0:NH, ch - 1:ch],
                    ident[0:NH, 0:NH]),
                    reads=[B_(f"ebt{bd}"), B_("ident")], writes=pw(6 + mbk, 32, NH))
                S.op("dve", lambda v: v.tensor_copy(erow3[0:1, 0:NH],
                                                    psum[6 + mbk][0:1, 32:32 + NH]),
                     reads=pw(6 + mbk, 32, NH), writes=[B_("erow3")])
                S.op("pe", lambda pe: pe.matmul(
                    psum[6 + mbk][:, 0:NH], ones[0:1, :], erow3[0:1, 0:NH],
                    start=True, stop=True),
                    reads=[B_("erow3"), B_("ones")], writes=pw(6 + mbk, 0, NH))
                S.op("dve", lambda v: v.tensor_copy(ecol[:, :], psum[6 + mbk][:, 0:NH]),
                     reads=pw(6 + mbk, 0, NH), writes=[bEc])
            S.op("dve", lambda v, bd=bd, ec=ec: v.tensor_tensor(
                scr3.rearrange("p (h q) -> p h q", h=NH),
                h0[bd].rearrange("p (h q) -> p h q", h=NH),
                ec.unsqueeze(2).broadcast_to((128, NH, HDIM)), OP.mult),
                reads=[B_(f"h0_{bd}"), bEc], writes=[B_("scr3")])
            S.op("dve", lambda v: v.tensor_tensor(hle[d][:, :], hle[d][:, :],
                                                  scr3[:, :], OP.add),
                 reads=[B_("scr3")], writes=[bHle])
            S.op("spd", lambda p: p.dma_start(out=cch[d][:, :],
                                              in_=sc_c[bd, :, Q * ch:Q * ch + Q]),
                 reads=[B_(f"scc{bd}")], writes=[B_("cch" + d)])
            for half in range(2):
                S.op("pe", lambda pe, half=half, mbk=mbk: pe.matmul(
                    psum[2 * half + mbk][:, 0:512], cch[d][:, :],
                    hle[d][:, 512 * half:512 * half + 512], start=True, stop=True),
                    reads=[B_("cch" + d), bHle], writes=pw(2 * half + mbk, 0, 512))
            S.op("spd", lambda p: p.dma_start(out=yld[d][:, :], in_=sc_y[bd, ch, :, :]),
                 reads=[B_(f"scy{bd}_{ch}")], writes=[B_("yld" + d)])
            for half in range(2):
                S.op("dve", lambda v, half=half, bd=bd, ch=ch, mbk=mbk: v.tensor_tensor(
                    ycb[d][:, 512 * half:512 * half + 512]
                        .rearrange("p (h q) -> p h q", h=8),
                    psum[2 * half + mbk][:, 0:512].rearrange("p (h q) -> p h q", h=8),
                    edel[bd][ch][:, 8 * half:8 * half + 8]
                        .unsqueeze(2).broadcast_to((128, 8, HDIM)), OP.mult),
                    reads=pw(2 * half + mbk, 0, 512) + [B_(f"edel{bd}_{ch}")],
                    writes=[B_("ycb" + d)])
            S.op("dve", lambda v: v.tensor_tensor(ycb[d][:, :], ycb[d][:, :],
                                                  yld[d][:, :], OP.add),
                 reads=[B_("yld" + d)], writes=[B_("ycb" + d)])
            for fc in range(8):
                S.op("spd", lambda p, fc=fc: p.dma_start(
                    out=zld[:, 128 * fc:128 * fc + 128],
                    in_=sc_z[bd, fc, :, Q * ch:Q * ch + Q]),
                    reads=[B_(f"scz{bd}_{fc}")], writes=[B_(f"zld{fc}")])
            for fc in range(8):
                bank = 4 + mbk
                S.op("pe", lambda pe, fc=fc, bank=bank: pe.transpose(
                    psum[bank][:, 0:128], zld[:, 128 * fc:128 * fc + 128], ident[:, :]),
                    reads=[B_(f"zld{fc}"), B_("ident")], writes=pw(bank, 0, 128))
                S.op("dve", lambda v, fc=fc, bank=bank: v.tensor_tensor(
                    ycb[d][:, 128 * fc:128 * fc + 128], psum[bank][:, 0:128],
                    ycb[d][:, 128 * fc:128 * fc + 128], OP.mult),
                    reads=pw(bank, 0, 128), writes=[B_("ycb" + d)])
            S.op("act", lambda a: a.activation(out=scr3[:, :], in_=ycb[d][:, :],
                                               func=AF.Square, accum_out=c3[:, 0:1]),
                 reads=[B_("ycb" + d)], writes=[B_("scr3"), B_("c3")])
            S.op("act", lambda a: a.activation(out=c3[:, 1:2], in_=c3[:, 0:1],
                                               func=AF.Sqrt, scale=1.0 / DIN,
                                               bias=epscol[:, :]),
                 reads=[B_("c3"), B_("epscol")], writes=[B_("c3")])
            S.op("dve", lambda v: v.reciprocal(c3[:, 1:2], c3[:, 1:2]),
                 reads=[B_("c3")], writes=[B_("c3")])
            S.op("dve", lambda v: v.scalar_tensor_tensor(
                ynm[d][:, :], ycb[d][:, :], c3[:, 1:2], gnw_bc[d][:, :],
                OP.mult, OP.mult),
                reads=[B_("ycb" + d), B_("c3"), B_("gnw" + d)], writes=[B_("ynm" + d)])
            for fc in range(8):
                bank = 4 + mbk
                S.op("pe", lambda pe, fc=fc, bank=bank: pe.transpose(
                    psum[bank][:, 128:256], ynm[d][:, 128 * fc:128 * fc + 128],
                    ident[:, :]),
                    reads=[B_("ynm" + d), B_("ident")], writes=pw(bank, 128, 128))
                dst = yfm[d][:, 128 * fc:128 * fc + 128]
                if d == "b":
                    dst = dst[:, ::-1]
                S.op("dve", lambda v, dst=dst, bank=bank: v.tensor_copy(
                    dst, psum[bank][:, 128:256]),
                    reads=pw(bank, 128, 128), writes=[B_("yfm" + d)])
            coff = 0 if d == "f" else 512
            for oc in range(4):
                pn = 2 + mbk
                for fc in range(8):
                    S.op("pe", lambda pe, fc=fc, oc=oc, pn=pn, d=d: pe.matmul(
                        psum[pn][:, 0:128],
                        outpw_s[d][fc][:, 128 * oc:128 * oc + 128],
                        yfm[d][:, 128 * fc:128 * fc + 128],
                        start=(fc == 0), stop=(fc == 7)),
                        reads=[B_("yfm" + d), B_(f"outpw{d}{fc}")],
                        writes=pw(pn, 0, 128))
                S.op("dve", lambda v, oc=oc, pn=pn, coff=coff: v.tensor_copy(
                    cat[:, coff + 128 * oc:coff + 128 * oc + 128],
                    psum[pn][:, 0:128]),
                    reads=pw(pn, 0, 128), writes=[B_("cat")])

        for b in range(B):
            for jf in range(NCH):
                member(b, "f", jf)
                member(b, "b", NCH - 1 - jf)
                g0 = b * tloc + Q * jf
                for fc in range(8):
                    S.op("spd", lambda p, fc=fc, g0=g0: p.dma_start(
                        out=xnld[:, 128 * fc:128 * fc + 128],
                        in_=sc_xn[fc, :, g0:g0 + Q]),
                        reads=[B_(f"scxn{fc}")], writes=[B_(f"xnld{fc}")])
                S.op("spd", lambda p, g0=g0: p.dma_start(
                    out=gld[:, :], in_=sc_g[g0:g0 + Q, :]),
                    reads=[B_(f"scgb{g0 // 128}")], writes=[B_("gld")])
                S.op("spd", lambda p, b=b, jf=jf: p.dma_start(
                    out=xld[:, :], in_=x_in[b, 3 + Q * jf:3 + Q * jf + Q, :]),
                    writes=[B_("xld")])
                for oc in range(8):
                    pn = oc % 2
                    for kc in range(8):
                        S.op("pe", lambda pe, oc=oc, kc=kc, pn=pn: pe.matmul(
                            psum[pn][:, 0:128], opw_s[kc][:, 128 * oc:128 * oc + 128],
                            cat[:, 128 * kc:128 * kc + 128], start=(kc == 0),
                            stop=(kc == 7)),
                            reads=[B_("cat"), B_(f"opw{kc}")], writes=pw(pn, 0, 128))
                    S.op("dve", lambda v, oc=oc, pn=pn: v.scalar_tensor_tensor(
                        sfm[:, 128 * oc:128 * oc + 128], psum[pn][:, 0:128],
                        opb_c[:, oc:oc + 1], xnld[:, 128 * oc:128 * oc + 128],
                        OP.add, OP.add),
                        reads=pw(pn, 0, 128) + [B_("opb"), B_(f"xnld{oc}")],
                        writes=[B_("sfm")])
                for fc in range(8):
                    bank = 4 + fc % 2
                    S.op("pe", lambda pe, fc=fc, bank=bank: pe.transpose(
                        psum[bank][:, 256:384], sfm[:, 128 * fc:128 * fc + 128],
                        ident[:, :]),
                        reads=[B_("sfm"), B_("ident")], writes=pw(bank, 256, 128))
                    S.op("dve", lambda v, fc=fc, bank=bank: v.tensor_tensor(
                        pfm[:, 128 * fc:128 * fc + 128], psum[bank][:, 256:384],
                        gld[:, 128 * fc:128 * fc + 128], OP.mult),
                        reads=pw(bank, 256, 128) + [B_("gld")], writes=[B_("pfm")])
                S.op("dve", lambda v: v.tensor_tensor(otm[:, :], pfm[:, :],
                                                      xld[:, :], OP.add),
                     reads=[B_("pfm"), B_("xld")], writes=[B_("otm")])
                oi = S.op("gpd", lambda p, b=b, jf=jf: p.dma_start(
                    out=out_ext[b, Q * jf:Q * jf + Q, :], in_=otm[:, :]),
                    reads=[B_("otm")], writes=[B_(f"outdma{b}_{jf}")])
                out_dmas.append(oi)
                if _dbgck and _dbgck.startswith("m,") and \
                        [b, jf] == [int(v) for v in _dbgck.split(",")[1:]]:
                    for di, (bn, ap) in enumerate([
                            ("cat", cat[:, :]), ("sfm", sfm[:, :]),
                            ("gld", gld[:, :]), ("xld", xld[:, :]),
                            ("xnld7", xnld[:, :]), ("otm", otm[:, :]),
                            ("ycbf", ycb["f"][:, :]), ("ycbb", ycb["b"][:, :]),
                            ("ynmf", ynm["f"][:, :]), ("ynmb", ynm["b"][:, :]),
                            ("yfmf", yfm["f"][:, :]), ("yfmb", yfm["b"][:, :]),
                            ("hlef", hle["f"][:, :]), ("yldf", yld["f"][:, :]),
                            ("zld", zld[:, :]), ("c3", c3[:, :])]):
                        S.op("gpd", lambda p, di=di, ap=ap: p.dma_start(
                            out=dbg_t[di, 0:ap.shape[0], 0:ap.shape[-1]], in_=ap),
                            reads=[B_(bn)], writes=[B_(f"dbgm{di}")])
        main.reset(m0)

    out_dmas = []
    setup_consts()
    S.fence()
    if "1" in phases:
        for b in range(B):
            phase1_pass(b, "f")
        for b in range(B):
            phase1_pass(b, "b")
        S.fence()
    S.op("gpd", lambda p: p.dma_start(out=ag_in[:, 4 * DIN:4 * DIN + 64],
                                      in_=gam_ag[:, :]),
         reads=[B_("gamag")], writes=[B_("agin")])
    if ncores_run > 1:
        S.op("gp", lambda g: g.collective_compute(
            "AllGather", OP.bypass, replica_groups=[list(range(ncores_run))],
            ins=[ag_in.opt()], outs=[ag_out.opt()]),
            reads=[B_("agin")], writes=[B_("agout")])
    else:
        S.op("gpd", lambda p: p.dma_start(out=ag_out[:, :], in_=ag_in[:, :]),
             reads=[B_("agin")], writes=[B_("agout")])
    if "2" in phases:
        phase2()
    S.fence()
    if "3" in phases:
        phase3()
    S.op("gp", lambda g: g.nop(), extra=list(out_dmas))

    sems = {}
    dsems = {}
    semcms = []
    for k in Sched.CKEYS:
        cm = nc.semaphore(f"sem_{k}")
        sems[k] = cm.__enter__()
        semcms.append(cm)
    for k in Sched.DKEYS:
        dsems[k] = []
        for s in range(Sched.NDSEM):
            cm = nc.semaphore(f"sem_{k}{s}")
            dsems[k].append(cm.__enter__())
            semcms.append(cm)
    with nc.Block() as block:
        S.emit(block, sems, dsems)
    for cm in reversed(semcms):
        cm.__exit__(None, None, None)
    return nc


_NC_CACHE = {}


def _get_nc(tloc, ncores):
    key = (tloc, ncores)
    if key not in _NC_CACHE:
        _NC_CACHE[key] = build_nc(tloc, ncores)
    return _NC_CACHE[key]


def make_in_maps(inputs, tloc, ncores):
    f32 = lambda a: np.ascontiguousarray(np.asarray(a), dtype=np.float32)
    x = f32(inputs["L0_attn_x"])
    T = x.shape[1]
    assert T == tloc * ncores
    base = {
        "inwT_f": f32(inputs["L0_attn_f_in_w"].T),
        "inwT_b": f32(inputs["L0_attn_b_in_w"].T),
        "convw_f": f32(inputs["L0_attn_f_conv_w"]),
        "convw_b": f32(inputs["L0_attn_b_conv_w"]),
        "convb_f": f32(np.reshape(inputs["L0_attn_f_conv_b"], (10, 128)).T),
        "convb_b": f32(np.reshape(inputs["L0_attn_b_conv_b"], (10, 128)).T),
        "dtb_f": f32(inputs["L0_attn_f_dt_bias"]).reshape(NH, 1),
        "dtb_b": f32(inputs["L0_attn_b_dt_bias"]).reshape(NH, 1),
        "alog_f": f32(inputs["L0_attn_f_A_log"]).reshape(NH, 1),
        "alog_b": f32(inputs["L0_attn_b_A_log"]).reshape(NH, 1),
        "dd_f": f32(inputs["L0_attn_f_D"]).reshape(1, NH),
        "dd_b": f32(inputs["L0_attn_b_D"]).reshape(1, NH),
        "gnw_f": f32(inputs["L0_attn_f_gnorm_w"]).reshape(1, DIN),
        "gnw_b": f32(inputs["L0_attn_b_gnorm_w"]).reshape(1, DIN),
        "outpwT_f": f32(inputs["L0_attn_f_outp_w"].T),
        "outpwT_b": f32(inputs["L0_attn_b_outp_w"].T),
        "normw": f32(inputs["L0_attn_norm_w"]).reshape(1, D),
        "opwT": f32(inputs["L0_attn_op_w"].T),
        "opb": f32(np.reshape(inputs["L0_attn_op_b"], (8, 128)).T),
        "gatewT": f32(inputs["L0_ffn_gate_w"].T),
        "ebias": f32(inputs["L0_ffn_expert_bias"]).reshape(ETOT, 1),
        "w1T": f32(np.transpose(inputs["L0_ffn_w1"], (0, 2, 1))),
        "w2T": f32(np.transpose(inputs["L0_ffn_w2"], (0, 2, 1))),
    }
    in_maps = []
    for c in range(ncores):
        xs = np.zeros((B, tloc + 6, D), np.float32)
        s0 = c * tloc
        s1 = s0 + tloc
        xs[:, 3:3 + tloc, :] = x[:, s0:s1, :]
        if s0 > 0:
            xs[:, 0:3, :] = x[:, s0 - 3:s0, :]
        if s1 < T:
            xs[:, 3 + tloc:, :] = x[:, s1:s1 + 3, :]
        mf = np.zeros((128, 8), np.float32)
        mb = np.zeros((128, 8), np.float32)
        if c > 0:
            mf[:, c - 1] = 1.0
        if c < ncores - 1:
            mb[:, c + 1] = 1.0
        m = dict(base)
        m["x"] = xs
        m["msnap_f"] = mf
        m["msnap_b"] = mb
        in_maps.append(m)
    return in_maps


def kernel(**inputs):
    ncores = NCORES
    T = np.asarray(inputs["L0_attn_x"]).shape[1]
    tloc = T // ncores
    nc = _get_nc(tloc, ncores)
    in_maps = make_in_maps(inputs, tloc, ncores)
    res = run_bass_kernel_spmd(nc, in_maps, core_ids=list(range(ncores)))
    return np.concatenate([r["out"] for r in res.results], axis=1)
